# version 03
# baseline (speedup 1.0000x reference)
import contextlib
from collections import defaultdict

import numpy as np
import concourse.bass as bass
import concourse.mybir as mybir
from concourse.bass_utils import run_bass_kernel_spmd

F32 = mybir.dt.float32
BF16 = mybir.dt.bfloat16
AF = mybir.ActivationFunctionType
ALU = mybir.AluOpType
AX = mybir.AxisListType

NCORE = 8
DM = 4096
NT = 2048
NCH = 512
NNC = NT // NCH
KD = DM // 128
S = 1024
FF = 9984
G = 6
RANK = 32
E = 8
NANCH = NT // 2

ENGINES = ("sync", "tensor", "vector", "scalar", "gpsimd")


def _prep_lhsT(wT):
    K, M = wT.shape
    assert K % 128 == 0
    kt = K // 128
    mt = min(128, M)
    assert M % mt == 0
    mtiles = M // mt
    a = wT.reshape(kt, 128, mtiles, mt).transpose(2, 1, 0, 3)
    return np.ascontiguousarray(a).reshape(mtiles, 128, kt * mt)


class Plan:

    def __init__(self):
        self.q = {e: [] for e in ENGINES}
        self.C = defaultdict(int)
        self.sems = {}

    def raw(self, eng, fn):
        self.q[eng].append(fn)

    def wait(self, eng, semname, val):
        if val <= 0:
            return
        self.q[eng].append(lambda E, s=semname, v=val: E.wait_ge(self.sems[s], v))

    def op(self, eng, fn, deps=(), inc=None, inc_by=1):
        if inc is None and eng in ("vector", "scalar"):
            inc = {"vector": "ov", "scalar": "os"}[eng]
            self.wait(eng, inc, self.C[inc])
        for d in deps:
            if d is None:
                continue
            s, v = d
            self.wait(eng, s, v)
        if inc is not None:
            self.C[inc] += inc_by
            val = self.C[inc]
            self.q[eng].append(lambda E, fn=fn, s=inc, v=inc_by: fn(E).then_inc(self.sems[s], v))
            return (inc, val)
        self.q[eng].append(lambda E, fn=fn: fn(E))
        return None

    def v(self, fn, deps=()):
        return self.op("vector", fn, deps)

    def s(self, fn, deps=()):
        return self.op("scalar", fn, deps)

    def t(self, fn, deps=(), inc="mm"):
        return self.op("tensor", fn, deps, inc=inc)

    def dma(self, eng, semname, fn, deps=()):
        return self.op(eng, fn, deps, inc=semname, inc_by=16)

    def now(self, semname):
        return (semname, self.C[semname])


def _finalize(nc, plan, es, semnames):
    for n in semnames:
        plan.sems[n] = es.enter_context(nc.semaphore(n))
    with nc.Block() as block:
        sections = {"sync": block.sync, "tensor": block.tensor, "vector": block.vector,
                    "scalar": block.scalar, "gpsimd": block.gpsimd}
        for name in ENGINES:
            def body(E, name=name):
                for fn in plan.q[name]:
                    fn(E)
            sections[name](body)



def build_l1():
    nc = bass.Bass("TRN2", target_bir_lowering=False)
    x = nc.dram_tensor("x", [DM, NT], F32, kind="ExternalInput").ap()
    WA = nc.dram_tensor("WA", [1, 128, KD * 40], F32, kind="ExternalInput").ap()
    bA = nc.dram_tensor("bA", [1, 40], F32, kind="ExternalInput").ap()
    limT = nc.dram_tensor("limT", [RANK, E], F32, kind="ExternalInput").ap()
    onescol = nc.dram_tensor("onescol", [128, 1], F32, kind="ExternalInput").ap()
    identin = nc.dram_tensor("identin", [128, 128], F32, kind="ExternalInput").ap()
    lnx1 = nc.dram_tensor("lnx1", [DM, NT], BF16, kind="ExternalOutput").ap()
    z1 = nc.dram_tensor("z1", [RANK, NT], F32, kind="ExternalOutput").ap()
    h8a = nc.dram_tensor("h8a", [E, NANCH], F32, kind="ExternalOutput").ap()
    dsl = nc.dram_tensor("dsl", [E, NANCH], F32, kind="ExternalOutput").ap()
    partials = nc.dram_tensor("partials", [1, 2], F32, kind="ExternalOutput").ap()

    P = Plan()
    C = P.C
    es = contextlib.ExitStack()
    with es:
        def sbuf(name, shape, dt):
            return es.enter_context(nc.sbuf_tensor(name, shape, dt))
        def psum(name, shape, dt=F32):
            return es.enter_context(nc.psum_tensor(name, shape, dt))

        xb = [sbuf(f"xb{i}", [128, NT], F32) for i in range(2)]
        acc = sbuf("acc", [128, NT], F32)
        acc2 = sbuf("acc2", [128, NT], F32)
        sq = [sbuf(f"sq{i}", [128, NT], F32) for i in range(2)]
        onesc = sbuf("onesc", [128, 1], F32)
        ident = sbuf("ident", [128, 128], F32)
        ones1r = sbuf("ones1r", [1, 128], F32)
        onesN = sbuf("onesN", [1, NCH], F32)
        srow = sbuf("srow", [1, NT], F32)
        sqrow = sbuf("sqrow", [1, NT], F32)
        mean = sbuf("mean", [1, NT], F32)
        msq = sbuf("msq", [1, NT], F32)
        var = sbuf("var", [1, NT], F32)
        arow = sbuf("arow", [1, NT], F32)
        brow = sbuf("brow", [1, NT], F32)
        Ba = sbuf("Ba", [128, NT], F32)
        Bb = sbuf("Bb", [128, NT], F32)
        WAb = sbuf("WAb", [128, KD * 40], BF16)
        bAs = sbuf("bAs", [1, 40], F32)
        limTs = sbuf("limTs", [RANK, E], F32)
        lnxb = [sbuf(f"lnxb{i}", [128, NT], BF16) for i in range(2)]
        tb = [sbuf(f"tb{i}", [128, NT], F32) for i in range(2)]
        zH = sbuf("zH", [40, NT], F32)
        h8s = sbuf("h8s", [E, NANCH], F32)
        dsls = sbuf("dsls", [E, NANCH], F32)
        hm8 = sbuf("hm8", [E, 1], F32)
        dm8 = sbuf("dm8", [E, 1], F32)
        hm1 = sbuf("hm1", [1, E], F32)
        dm1 = sbuf("dm1", [1, E], F32)
        parts = sbuf("parts", [1, 2], F32)

        p_bc = [psum(f"p_bc{i}", [128, NCH]) for i in range(2)]
        p_wa = [psum(f"p_wa{i}", [40, NCH]) for i in range(4)]
        p_sl = psum("p_sl", [8, NCH])
        p_tr = psum("p_tr", [1, 8])

        for src, dst in ((onescol[:, :], onesc[:]), (identin[:, :], ident[:]),
                         (bA[:, :], bAs[:]), (limT[:, :], limTs[:])):
            P.dma("sync", "c", lambda E, s=src, d=dst: E.dma_start(out=d, in_=s))
        c_all = P.now("c")
        P.dma("gpsimd", "w", lambda E: E.dma_start(out=WAb[:], in_=WA[0]))
        w_all = P.now("w")
        m1 = P.v(lambda E: E.memset(ones1r[:], 1.0))
        m2 = P.v(lambda E: E.memset(onesN[:], 1.0))

        vacc = {}
        for k in range(KD):
            i = k % 2
            deps = [vacc[k - 2]] if k >= 2 else []
            xk = P.dma("sync", f"x{i}", lambda E, k=k, i=i: E.dma_start(
                out=xb[i][:], in_=x[k * 128:(k + 1) * 128, :]), deps=deps)
            sqk = P.s(lambda E, i=i: E.activation(sq[i][:], xb[i][:], AF.Square),
                      deps=[xk] + ([vacc[k - 2]] if k >= 2 else []))
            if k == 0:
                P.v(lambda E: E.tensor_copy(acc[:], xb[0][:]), deps=[xk])
                vacc[k] = P.v(lambda E: E.tensor_copy(acc2[:], sq[0][:]), deps=[sqk])
            else:
                P.v(lambda E, i=i: E.tensor_add(acc[:], acc[:], xb[i][:]), deps=[xk])
                vacc[k] = P.v(lambda E, i=i: E.tensor_add(acc2[:], acc2[:], sq[i][:]), deps=[sqk])

        st_copy = {}
        for j in range(2 * NNC):
            n, is_sq = j // 2, j % 2
            pb = p_bc[j % 2]
            srcacc = acc2 if is_sq else acc
            dstrow = sqrow if is_sq else srow
            deps = [c_all, vacc[KD - 1]] + ([st_copy[j - 2]] if j >= 2 else [])
            mmj = P.t(lambda E, n=n, pb=pb, a=srcacc: E.matmul(
                pb[0:1, :], onesc[:], a[:, n * NCH:(n + 1) * NCH], start=True, stop=True), deps=deps)
            st_copy[j] = P.v(lambda E, n=n, pb=pb, d=dstrow: E.tensor_copy(
                d[:, n * NCH:(n + 1) * NCH], pb[0:1, :]), deps=[mmj])
        last_st = st_copy[2 * NNC - 1]
        P.v(lambda E: E.tensor_scalar_mul(mean[:], srow[:], 1.0 / DM), deps=[last_st])
        P.v(lambda E: E.tensor_scalar_mul(var[:], sqrow[:], 1.0 / DM))
        P.v(lambda E: E.tensor_mul(msq[:], mean[:], mean[:]))
        P.v(lambda E: E.tensor_sub(var[:], var[:], msq[:]))
        veps = P.v(lambda E: E.tensor_scalar_add(var[:], var[:], 1e-5))
        sln = P.s(lambda E: E.activation(arow[:], var[:], AF.Ln), deps=[veps])
        sexp = P.s(lambda E: E.activation(arow[:], arow[:], AF.Exp, scale=-0.5), deps=[sln])
        vbrow = P.v(lambda E: E.scalar_tensor_tensor(brow[:], mean[:], -1.0, arow[:], ALU.mult, ALU.mult),
                    deps=[sexp])
        prev = {}
        jj = 0
        for (row, dst, dep) in ((arow, Ba, sexp), (brow, Bb, vbrow)):
            for n in range(NNC):
                pb = p_bc[jj % 2]
                deps = [dep, m1] + ([prev[jj - 2]] if jj >= 2 else [last_st])
                mmb = P.t(lambda E, row=row, n=n, pb=pb: E.matmul(
                    pb[:], ones1r[:], row[:, n * NCH:(n + 1) * NCH], start=True, stop=True), deps=deps)
                prev[jj] = P.s(lambda E, dst=dst, n=n, pb=pb: E.copy(
                    dst[:, n * NCH:(n + 1) * NCH], pb[:]), deps=[mmb])
                jj += 1
        bcast_fin = prev[jj - 1]

        vln, mm_wa, stln = {}, {}, {}
        for k in range(KD):
            i = k % 2
            deps = [vacc[KD - 1]] + ([vln[k - 2]] if k >= 2 else [])
            xk = P.dma("sync", f"x{i}", lambda E, k=k, i=i: E.dma_start(
                out=xb[i][:], in_=x[k * 128:(k + 1) * 128, :]), deps=deps)
            vdeps = [xk, bcast_fin]
            if k >= 2:
                vdeps += [mm_wa[k - 2], stln[k - 2]]
            P.v(lambda E, i=i: E.tensor_mul(tb[i][:], xb[i][:], Ba[:]), deps=vdeps)
            vln[k] = P.v(lambda E, i=i: E.tensor_add(lnxb[i][:], tb[i][:], Bb[:]))
            for n in range(NNC):
                mm_wa[k] = P.t(lambda E, k=k, n=n, i=i: E.matmul(
                    p_wa[n][:], WAb[:, k * 40:(k + 1) * 40], lnxb[i][:, n * NCH:(n + 1) * NCH],
                    start=(k == 0), stop=False),
                    deps=([vln[k], w_all] if n == 0 else []))
            stln[k] = P.dma("sync", f"st{i}", lambda E, k=k, i=i: E.dma_start(
                out=lnx1[k * 128:(k + 1) * 128, :], in_=lnxb[i][:]), deps=[vln[k]])
        for n in range(NNC):
            P.t(lambda E, n=n: E.matmul(p_wa[n][:], bAs[:], onesN[:], start=False, stop=True),
                deps=([c_all, m2] if n == 0 else []))
        mm_wa_done = P.now("mm")

        ez = {}
        for n in range(NNC):
            ez[n] = P.v(lambda E, n=n: E.tensor_copy(zH[:, n * NCH:(n + 1) * NCH], p_wa[n][:]),
                        deps=[mm_wa_done] if n == 0 else [])
        vh8 = P.v(lambda E: E.tensor_copy(h8s[:], zH[32:40, 1::2]))
        dsl_ev = {}
        for half in range(2):
            mmd = P.t(lambda E, half=half: E.matmul(
                p_sl[:], limTs[:], zH[0:32, 1 + half * 1024:min(1 + half * 1024 + 1024, NT):2],
                start=True, stop=True),
                deps=([ez[NNC - 1], c_all] if half == 0 else [dsl_ev[0]]))
            dsl_ev[half] = P.v(lambda E, half=half: E.tensor_copy(
                dsls[:, half * NCH:(half + 1) * NCH], p_sl[:]), deps=[mmd])
        vm1 = P.v(lambda E: E.reduce_max(hm8[:], h8s[:], axis=AX.X, apply_absolute_value=True),
                  deps=[vh8])
        vm2 = P.v(lambda E: E.reduce_max(dm8[:], dsls[:], axis=AX.X, apply_absolute_value=True),
                  deps=[dsl_ev[1]])
        mtr1 = P.t(lambda E: E.matmul(p_tr[:], hm8[:], ident[0:8, 0:8], start=True, stop=True),
                   deps=[vm1])
        vtr1 = P.v(lambda E: E.tensor_copy(hm1[:], p_tr[0:1, :]), deps=[mtr1])
        mtr2 = P.t(lambda E: E.matmul(p_tr[:], dm8[:], ident[0:8, 0:8], start=True, stop=True),
                   deps=[vm2, vtr1])
        vtr2 = P.v(lambda E: E.tensor_copy(dm1[:], p_tr[0:1, :]), deps=[mtr2])
        P.v(lambda E: E.reduce_max(parts[:, 0:1], hm1[:], axis=AX.X), deps=[vtr1])
        vparts = P.v(lambda E: E.reduce_max(parts[:, 1:2], dm1[:], axis=AX.X), deps=[vtr2])

        P.dma("sync", "so", lambda E: E.dma_start(out=z1[:, :], in_=zH[0:32, :]), deps=[ez[NNC - 1]])
        P.dma("sync", "so", lambda E: E.dma_start(out=h8a[:, :], in_=h8s[:]), deps=[vh8])
        P.dma("sync", "so", lambda E: E.dma_start(out=dsl[:, :], in_=dsls[:]), deps=[dsl_ev[1]])
        P.dma("sync", "so", lambda E: E.dma_start(out=partials[:, :], in_=parts[:]), deps=[vparts])
        P.wait("sync", "so", C["so"])
        P.wait("sync", "st0", C["st0"])
        P.wait("sync", "st1", C["st1"])

        semnames = ("c", "w", "x0", "x1", "st0", "st1", "so", "ov", "os", "mm")
        _finalize(nc, P, es, semnames)
    return nc



def _host_prep(inputs):
    p = {}
    x = np.asarray(inputs["L0_attn_hidden_states"])
    xt = np.ascontiguousarray(x.reshape(16384, DM).T)
    p["x_shards"] = [np.ascontiguousarray(xt[:, c * NT:(c + 1) * NT]) for c in range(NCORE)]

    W = np.asarray(inputs["L0_ffn_W"])
    A = np.asarray(inputs["L0_ffn_A"])
    b = np.asarray(inputs["L0_ffn_b"]).astype(np.float32)
    Wt = np.ascontiguousarray(W.T)
    WA = np.concatenate([A, Wt[:, 0:8]], axis=1)
    p["WA"] = _prep_lhsT(WA)
    p["bA"] = np.concatenate([np.zeros(32, np.float32), b[:8]]).reshape(1, 40)
    p["limT"] = np.ascontiguousarray(np.asarray(inputs["L0_ffn_LiMEs"]).T)
    p["onescol"] = np.ones((128, 1), np.float32)
    p["identin"] = np.eye(128, dtype=np.float32)
    p["Wp"] = _prep_lhsT(Wt)
    p["bcolp"] = np.ascontiguousarray(b.reshape(KD, 128).T)
    p["Bm"] = np.ascontiguousarray(np.asarray(inputs["L0_ffn_Bmat"]))
    p["lim"] = np.ascontiguousarray(np.asarray(inputs["L0_ffn_LiMEs"]))
    upP = _prep_lhsT(np.ascontiguousarray(np.asarray(inputs["L0_attn_up_w"]).T))
    gateP = _prep_lhsT(np.ascontiguousarray(np.asarray(inputs["L0_attn_gate_w"]).T))
    p["UGp"] = np.ascontiguousarray(np.concatenate([upP, gateP], axis=2))
    p["lpc"] = np.ascontiguousarray(np.asarray(inputs["L0_attn_local_point"]).reshape(E, 128).T)
    p["disc"] = np.asarray(inputs["L0_attn_discount_factor"]).reshape(1, G).astype(np.float32)
    p["FGp"] = _prep_lhsT(np.ascontiguousarray(np.asarray(inputs["L0_attn_ffn_gate_w"]).T))
    p["FUp"] = _prep_lhsT(np.ascontiguousarray(np.asarray(inputs["L0_attn_ffn_up_w"]).T))
    lao = np.asarray(inputs["L0_attn_local_approx_output"])
    fuse = np.asarray(inputs["L0_attn_fuse_weight"])
    parts = [_prep_lhsT(np.ascontiguousarray(lao.T))]
    for i in range(G):
        parts.append(_prep_lhsT(np.ascontiguousarray(fuse[:, :, i].T)))
    parts.append(_prep_lhsT(np.ascontiguousarray(np.asarray(inputs["L0_attn_ffn_down_w"]).T)))
    p["WB"] = np.ascontiguousarray(np.concatenate(parts, axis=2))
    return p


def kernel(**inputs):
    p = _host_prep(inputs)
    nc1 = build_l1()
    in_maps = [{"x": p["x_shards"][c], "WA": p["WA"], "bA": p["bA"],
                "limT": p["limT"], "onescol": p["onescol"], "identin": p["identin"]}
               for c in range(NCORE)]
    r1 = run_bass_kernel_spmd(nc1, in_maps, core_ids=list(range(NCORE)))
    kernel._l1 = r1
    pall = np.concatenate([r1.results[c]["partials"] for c in range(NCORE)], axis=0)
    pall = np.ascontiguousarray(pall)
    nc2 = build_l2()
    in_maps2 = [{
        "x": p["x_shards"][c], "lnx1": r1.results[c]["lnx1"], "z1": r1.results[c]["z1"],
        "h8a": r1.results[c]["h8a"], "dsl": r1.results[c]["dsl"], "pall": pall,
        "Wp": p["Wp"], "bcolp": p["bcolp"], "Bm": p["Bm"], "lim": p["lim"],
        "UGp": p["UGp"], "lpc": p["lpc"], "disc": p["disc"],
        "FGp": p["FGp"], "FUp": p["FUp"], "WB": p["WB"],
        "WA": p["WA"], "bA": p["bA"], "limT": p["limT"],
        "onescol": p["onescol"], "identin": p["identin"],
    } for c in range(NCORE)]
    r2 = run_bass_kernel_spmd(nc2, in_maps2, core_ids=list(range(NCORE)))
    kernel._l2 = r2
    pall3 = np.ascontiguousarray(
        np.concatenate([r2.results[c]["partials3"] for c in range(NCORE)], axis=0))
    nc3 = build_l2(mode="l3")
    in_maps3 = [{
        "x": r2.results[c]["x2"], "lnx1": r2.results[c]["lnx3"], "z1": r2.results[c]["z3"],
        "h8a": r2.results[c]["h8a3"], "dsl": r2.results[c]["dsl3"], "pall": pall3,
        "Wp": p["Wp"], "bcolp": p["bcolp"], "Bm": p["Bm"], "lim": p["lim"],
        "onescol": p["onescol"], "identin": p["identin"],
    } for c in range(NCORE)]
    r3 = run_bass_kernel_spmd(nc3, in_maps3, core_ids=list(range(NCORE)))
    kernel._l3 = r3
    full = np.concatenate([r3.results[c]["out"] for c in range(NCORE)], axis=1)
    return np.ascontiguousarray(full.T).reshape(2, 8192, DM)


NK_BIG = 134
DEBUG_L2 = False
NWH = 67
UGW = 64


def build_l2(phases=7, mode="l2"):
    nc = bass.Bass("TRN2", target_bir_lowering=False)
    dram = {}
    def din(name, shape, dt=F32):
        dram[name] = nc.dram_tensor(name, shape, dt, kind="ExternalInput").ap()
        return dram[name]
    def dout(name, shape, dt=F32):
        dram[name] = nc.dram_tensor(name, shape, dt, kind="ExternalOutput").ap()
        return dram[name]

    full = (mode == "l2")
    x = din("x", [DM, NT])
    lnx1 = din("lnx1", [DM, NT], BF16)
    z1 = din("z1", [RANK, NT])
    h8a = din("h8a", [E, NANCH])
    dsla = din("dsl", [E, NANCH])
    pall = din("pall", [E, 2])
    Wp = din("Wp", [KD, 128, KD * 128])
    bcolp = din("bcolp", [128, KD])
    Bm = din("Bm", [RANK, DM])
    lim = din("lim", [E, RANK])
    onescol = din("onescol", [128, 1])
    identin = din("identin", [128, 128])
    if full:
        UGp = din("UGp", [8, 128, KD * 2 * 128])
        lpc = din("lpc", [128, E])
        disc = din("disc", [1, G])
        FGp = din("FGp", [FF // 128, 128, KD * 128])
        FUp = din("FUp", [FF // 128, 128, KD * 128])
        WB = din("WB", [KD, 128, NK_BIG * 128])
        WA = din("WA", [1, 128, KD * 40])
        bA = din("bA", [1, 40])
        limT = din("limT", [RANK, E])
        x2o = dout("x2", [DM, NT])
        lnx3 = dout("lnx3", [DM, NT], BF16)
        z3 = dout("z3", [RANK, NT])
        h8a3 = dout("h8a3", [E, NANCH])
        dsl3 = dout("dsl3", [E, NANCH])
        partials3 = dout("partials3", [1, 2])
        x1d = nc.dram_tensor("x1d", [DM, NT], F32,
                             kind="ExternalOutput" if DEBUG_L2 else "Internal").ap()
        HID = nc.dram_tensor("HID", [NK_BIG, 128, NT], BF16, kind="Internal").ap()
    else:
        x1d = dout("out", [DM, NT])

    P = Plan()
    C = P.C
    es = contextlib.ExitStack()
    with es:
        big = es.enter_context(nc.sbuf_tensor("big", [128, 68608], BF16))
        scrF = es.enter_context(nc.sbuf_tensor("scrF", [128, 12288], F32))
        smla = es.enter_context(nc.sbuf_tensor("smla", [128, 3072], F32))
        pb = [es.enter_context(nc.psum_tensor(f"pb{i}", [128, NCH], F32)) for i in range(8)]

        def scr(c0, cols, parts=128, p0=0):
            return scrF[p0:p0 + parts, c0:c0 + cols]

        def scrB(c0, cols_bf16, parts=128, p0=0):
            assert cols_bf16 % 2 == 0
            return scrF[p0:p0 + parts, c0:c0 + cols_bf16 // 2].bitcast(BF16)

        def lnres(k):
            return big[:, k * NT:(k + 1) * NT]

        def hidres(kk):
            return big[:, kk * NCH:(kk + 1) * NCH]

        sm_ones1r = smla[0:1, 0:128]
        sm_onesN = smla[0:1, 128:640]
        sm_onesc = smla[0:128, 640:641]
        sm_ident = smla[0:128, 644:772]
        sm_hmax = smla[0:1, 772:780]
        sm_dmax = smla[0:1, 780:788]
        sm_hinv = smla[0:1, 788:789]
        sm_dinv = smla[0:1, 790:791]
        sm_s1c = smla[0:128, 792:793]
        sm_s2c = smla[0:128, 794:795]
        sm_ccol = smla[0:128, 796:802]
        sm_disc = smla[0:1, 802:808]
        sm_coef = smla[0:1, 808:814]
        sm_lp = smla[0:1, 816:1840]
        sm_zpb = smla[0:32, 1840:2864].bitcast(BF16)
        sm_limb = smla[0:8, 2864:2880].bitcast(BF16)
        sm_bAs = smla[0:1, 2880:2920]
        sm_limT = smla[0:32, 2920:2928]
        sm_parts = smla[0:1, 2930:2932]
        sm_pall = smla[0:8, 2932:2934]
        sm_pcol = smla[0:8, 2936:2937]
        sm_tm8 = smla[0:8, 2938:2939]
        sm_tm8b = smla[0:8, 2940:2941]
        sm_bcol = smla[0:128, 2944:2976]
        sm_onescb = smla[0:128, 2942:2943].bitcast(BF16)[:, 0:1]
        sm_lpc = smla[0:128, 2976:2984]

        def dma(eng, semname, fn, deps=()):
            return P.dma(eng, semname, fn, deps)

        cl = [(onescol[:, :], sm_onesc), (identin[:, :], sm_ident),
              (pall[:, :], sm_pall), (bcolp[:, :], sm_bcol)]
        if full:
            cl += [(bA[:, :], sm_bAs), (limT[:, :], sm_limT),
                   (disc[:, :], sm_disc), (lpc[:, :], sm_lpc)]
        for src, dst in cl:
            dma("sync", "c", lambda E, s=src, d=dst: E.dma_start(out=d, in_=s))
        c_all = P.now("c")
        dma("gpsimd", "cw", lambda E: E.dma_start(out=sm_limb, in_=lim[:, :]))
        cw_all = P.now("cw")
        m_ones = P.v(lambda E: E.memset(sm_ones1r, 1.0))
        m_ones2 = P.v(lambda E: E.memset(sm_onesN, 1.0))
        P.v(lambda E: E.memset(sm_onescb, 1.0))

        for k in range(KD):
            dma("sync", "lnr", lambda E, k=k: E.dma_start(
                out=lnres(k), in_=lnx1[k * 128:(k + 1) * 128, :]))
        lnr_all = P.now("lnr")

        g_z1s = scr(0, NT, 32)
        g_h8 = scr(2048, NANCH, 8)
        g_dsl = scr(3072, NANCH, 8)
        g_pmix = scr(4096, NANCH, 32)
        g_wT = scr(5120, NANCH, 8)
        g_wTb = scrB(6144, 2048, 8)
        gt = lambda j: scr(7168 + j * 16, 8)
        gc = lambda j: scr(7600 + j * 4, 1)

        ld_z1 = dma("sync", "g0", lambda E: E.dma_start(out=g_z1s, in_=z1[:, :]))
        ld_h8 = dma("sync", "g0", lambda E: E.dma_start(out=g_h8, in_=h8a[:, :]))
        ld_dsl = dma("sync", "g0", lambda E: E.dma_start(out=g_dsl, in_=dsla[:, :]))
        g0_all = P.now("g0")

        mt1 = P.t(lambda E: E.matmul(pb[0][0:1, 0:8], sm_pall[:, 0:1], sm_ident[0:8, 0:8],
                                     start=True, stop=True), deps=[c_all])
        vt1 = P.v(lambda E: E.tensor_copy(sm_hmax, pb[0][0:1, 0:8]), deps=[mt1])
        mt2 = P.t(lambda E: E.matmul(pb[1][0:1, 0:8], sm_pall[:, 1:2], sm_ident[0:8, 0:8],
                                     start=True, stop=True), deps=[vt1])
        vt2 = P.v(lambda E: E.tensor_copy(sm_dmax, pb[1][0:1, 0:8]), deps=[mt2])
        P.v(lambda E: E.reduce_max(sm_hinv, sm_hmax, axis=AX.X))
        P.v(lambda E: E.reduce_max(sm_dinv, sm_dmax, axis=AX.X))
        P.v(lambda E: E.tensor_scalar_max(sm_hinv, sm_hinv, 1e-6))
        P.v(lambda E: E.tensor_scalar_max(sm_dinv, sm_dinv, 1e-6))
        P.v(lambda E: E.reciprocal(sm_hinv, sm_hinv))
        P.v(lambda E: E.reciprocal(sm_dinv, sm_dinv))
        P.v(lambda E: E.tensor_scalar_mul(sm_hinv, sm_hinv, 0.5))
        vsc = P.v(lambda E: E.tensor_scalar_mul(sm_dinv, sm_dinv, 0.5))
        mb1 = P.t(lambda E: E.matmul(pb[2][:, 0:1], sm_ones1r, sm_hinv, start=True, stop=True),
                  deps=[vsc, m_ones])
        sb1 = P.s(lambda E: E.copy(sm_s1c, pb[2][:, 0:1]), deps=[mb1])
        mb2 = P.t(lambda E: E.matmul(pb[3][:, 0:1], sm_ones1r, sm_dinv, start=True, stop=True),
                  deps=[sb1])
        sb2 = P.s(lambda E: E.copy(sm_s2c, pb[3][:, 0:1]), deps=[mb2])
        scoef = P.s(lambda E: E.activation(sm_coef, sm_disc, AF.Exp, scale=-1.0), deps=[c_all]) if full else None
        ccps = []
        for m in range(G if full else 0):
            mc = P.t(lambda E, m=m: E.matmul(pb[4][:, 0:1], sm_ones1r, sm_coef[0:1, m:m + 1],
                                             start=True, stop=True),
                     deps=[scoef, m_ones] + ([ccps[-1]] if ccps else []))
            ccps.append(P.s(lambda E, m=m: E.copy(sm_ccol[:, m:m + 1], pb[4][:, 0:1]), deps=[mc]))
        ccol_fin = ccps[-1] if full else None
        s_nlp = P.s(lambda E: E.mul(sm_lpc, sm_lpc, -1.0), deps=[c_all]) if full else None

        wtr = []
        for cchunk in range(8):
            cs = slice(cchunk * 128, (cchunk + 1) * 128)
            mh = P.t(lambda E, cs=cs: E.matmul(pb[5][:, 0:8], g_h8[:, cs], sm_ident[0:8, 0:8],
                                               start=True, stop=True),
                     deps=[g0_all, c_all] + ([wtr[-1]] if wtr else []))
            vh = P.v(lambda E: E.tensor_copy(gt(0), pb[5][:, 0:8]), deps=[mh])
            md = P.t(lambda E, cs=cs: E.matmul(pb[6][:, 0:8], g_dsl[:, cs], sm_ident[0:8, 0:8],
                                               start=True, stop=True), deps=[vh])
            vd = P.v(lambda E: E.tensor_copy(gt(1), pb[6][:, 0:8]), deps=[md])
            P.v(lambda E: E.tensor_scalar_mul(gt(2), gt(1), sm_s2c[:, 0:1]), deps=[sb2])
            P.v(lambda E: E.scalar_tensor_tensor(gt(2), gt(0), sm_s1c[:, 0:1], gt(2),
                                                 ALU.mult, ALU.add))
            P.v(lambda E: E.reduce_max(gc(0), gt(2), axis=AX.X))
            vexp_in = P.v(lambda E: E.tensor_scalar(gt(3), gt(2), gc(0), None, ALU.subtract))
            sexp = P.s(lambda E: E.activation(gt(3), gt(3), AF.Exp), deps=[vexp_in])
            P.v(lambda E: E.reduce_sum(gc(1), gt(3), axis=AX.X), deps=[sexp])
            P.v(lambda E: E.reciprocal(gc(1), gc(1)))
            P.v(lambda E: E.tensor_scalar_mul(gt(4), gt(3), gc(1)))
            P.v(lambda E: E.reduce_max(gc(2), gt(4), axis=AX.X))
            P.v(lambda E: E.tensor_scalar(gt(5), gt(4), gc(2), None, ALU.is_ge))
            P.v(lambda E: E.reduce_sum(gc(3), gt(5), axis=AX.X))
            P.v(lambda E: E.scalar_tensor_tensor(gt(6), gt(5), -1e30, gt(4), ALU.mult, ALU.add))
            P.v(lambda E: E.reduce_max(gc(4), gt(6), axis=AX.X))
            P.v(lambda E: E.tensor_scalar(gc(5), gc(3), 2.0, None, ALU.is_ge))
            P.v(lambda E: E.tensor_sub(gc(6), gc(2), gc(4)))
            P.v(lambda E: E.tensor_mul(gc(6), gc(5), gc(6)))
            P.v(lambda E: E.tensor_add(gc(6), gc(6), gc(4)))
            vth = P.v(lambda E: E.tensor_scalar(gt(7), gt(4), gc(6), 2.0, ALU.subtract, ALU.mult))
            ssig = P.s(lambda E: E.activation(gt(7), gt(7), AF.Sigmoid), deps=[vth])
            P.v(lambda E: E.tensor_mul(gt(8), gt(4), gt(7)), deps=[ssig])
            P.v(lambda E: E.reduce_sum(gc(7), gt(8), axis=AX.X))
            P.v(lambda E: E.tensor_scalar_add(gc(7), gc(7), 1e-9))
            P.v(lambda E: E.reciprocal(gc(7), gc(7)))
            vw = P.v(lambda E: E.tensor_scalar_mul(gt(8), gt(8), gc(7)))
            mw = P.t(lambda E: E.matmul(pb[7][0:8, 0:128], gt(8), sm_ident[:, :],
                                        start=True, stop=True), deps=[vw])
            wtr.append(P.v(lambda E, cs=cs: E.tensor_copy(g_wT[:, cs], pb[7][0:8, 0:128]),
                           deps=[mw]))
        vwb = P.v(lambda E: E.tensor_copy(g_wTb[:, 0:NANCH], g_wT), deps=[wtr[-1]])
        pm_ev = []
        for half in range(2):
            mp = P.t(lambda E, half=half: E.matmul(
                pb[0][0:32, :], sm_limb, g_wTb[:, half * NCH:(half + 1) * NCH],
                start=True, stop=True), deps=[vwb, cw_all] + ([pm_ev[-1]] if pm_ev else []))
            pm_ev.append(P.v(lambda E, half=half: E.tensor_copy(
                g_pmix[:, half * NCH:(half + 1) * NCH], pb[0][0:32, :]), deps=[mp]))
        zpb_fin = None
        for par in range(2):
            zpb_fin = P.v(lambda E, par=par: E.scalar_tensor_tensor(
                sm_zpb[:, par::2], g_z1s[:, par::2], 0.5, g_pmix, ALU.mult, ALU.mult),
                deps=[pm_ev[-1], g0_all])

        p1_xres = [scr(0, NT), scr(2048, NT)]
        p1_obuf = scr(4096, NT)
        p1_wb = [scrB(6144, KD * 128), scrB(8192, KD * 128)]
        p1_Bmb = scrB(10240, DM, 32)

        dma("gpsimd", "cw2", lambda E: E.dma_start(out=p1_Bmb, in_=Bm[:, :]))
        cw2_all = P.now("cw2")

        ev_prev = {}
        st_prev = {}
        mmg = {}
        for m in range(KD):
            i = m % 2
            wdep = [mmg[(m - 2, NNC - 1)]] if m >= 2 else [zpb_fin]
            wk = dma("gpsimd", f"w{i}", lambda E, m=m, i=i: E.dma_start(
                out=p1_wb[i][:, :], in_=Wp[m]), deps=wdep)
            xk = dma("sync", f"x{i}", lambda E, m=m, i=i: E.dma_start(
                out=p1_xres[i][:, :], in_=x[m * 128:(m + 1) * 128, :]),
                deps=[ev_prev.get((m - 2, NNC - 1))] if m >= 2 else [zpb_fin])
            for n in range(NNC):
                bank = (m * NNC + n) % 8
                mmdeps = []
                if m == 0 and n == 0:
                    mmdeps = [lnr_all, wk, c_all, m_ones2]
                elif n == 0:
                    mmdeps = [wk]
                if (m * NNC + n) >= 8:
                    prev_grp = m * NNC + n - 8
                    mmdeps.append(ev_prev.get((prev_grp // NNC, prev_grp % NNC)))
                for k in range(KD):
                    P.t(lambda E, m=m, n=n, k=k, i=i, bank=bank: E.matmul(
                        pb[bank][:, :], p1_wb[i][:, k * 128:(k + 1) * 128],
                        lnres(k)[:, n * NCH:(n + 1) * NCH],
                        start=(k == 0), stop=False), deps=mmdeps if k == 0 else [])
                mmg[(m, n)] = P.t(lambda E, m=m, n=n, bank=bank: E.matmul(
                    pb[bank][:, :], p1_Bmb[:, m * 128:(m + 1) * 128],
                    sm_zpb[:, n * NCH:(n + 1) * NCH], start=False, stop=True),
                    deps=[zpb_fin, cw2_all] if (m == 0 and n == 0) else [])
                sev = P.v(lambda E, m=m, n=n, bank=bank: E.tensor_scalar(
                    p1_obuf[:, n * NCH:(n + 1) * NCH], pb[bank][:, :],
                    sm_bcol[:, m:m + 1], 0.5, ALU.add, ALU.mult),
                    deps=[mmg[(m, n)], c_all] + ([st_prev[m - 1]] if (n == 0 and m >= 1) else []))
                ev_prev[(m, n)] = P.v(lambda E, m=m, n=n, i=i: E.tensor_add(
                    p1_obuf[:, n * NCH:(n + 1) * NCH], p1_obuf[:, n * NCH:(n + 1) * NCH],
                    p1_xres[i][:, n * NCH:(n + 1) * NCH]),
                    deps=[xk])
            st_prev[m] = dma("sync", f"sx{m % 2}", lambda E, m=m: E.dma_start(
                out=x1d[m * 128:(m + 1) * 128, :], in_=p1_obuf[:, :]),
                deps=[ev_prev[(m, NNC - 1)]])
        ph1_done_st = [P.now("sx0"), P.now("sx1")]
        ph1_done_mm = P.now("mm")

        if phases <= 1 or not full:
            P.wait("sync", "sx0", C["sx0"])
            P.wait("sync", "sx1", C["sx1"])
            semnames = ["c", "cw", "cw2", "lnr", "g0", "w0", "w1", "x0", "x1",
                        "sx0", "sx1", "ov", "os", "mm"]
            _finalize(nc, P, es, semnames)
            return nc

        ph2 = emit_prepass(nc, P, pb, scrF, smla, x1d, lnto_big=lnres,
                           deps_start=[ph1_done_st[0], ph1_done_st[1], ph1_done_mm,
                                       ("ov", C["ov"]), c_all],
                           tag="p2", sm_onesc=sm_onesc, sm_ones1r=sm_ones1r,
                           sm_onescb=sm_onescb)

        ug = [scrB(0, KD * 2 * 128), scrB(4096, KD * 2 * 128)]
        sA = scr(8192, NCH)
        sAb = scrB(8192, NCH)
        sB = scr(8704, NCH)
        sC = scr(9216, NCH)
        sD = scr(9728, NCH)
        sE = scr(10240, NCH)
        sF = scr(10752, NCH)
        tstg = [scrB(11264, NCH), scrB(11520, NCH), None, None, None, None]
        ph2done = ph2["done"]
        tstg[2] = smla[0:128, 1840:2096].bitcast(BF16)
        tstg[3] = smla[0:128, 2096:2352].bitcast(BF16)
        tstg[4] = smla[0:128, 2352:2608].bitcast(BF16)
        tstg[5] = smla[0:128, 2608:2864].bitcast(BF16)
        mm_ph3 = {}
        ev_ph3 = {}
        tst_cnt = [0] * 6
        au_st = {}
        for ms in range(8):
            iu = ms % 2
            wdeps = [ph2done] if ms < 2 else [mm_ph3[(ms - 2, NNC - 1)]]
            ugk = P.dma("gpsimd", f"ug{iu}", lambda E, ms=ms, iu=iu: E.dma_start(
                out=ug[iu][:, :], in_=UGp[ms]), deps=wdeps)
            for n in range(NNC):
                j = ms * NNC + n
                bau, bgp = (2 * j) % 8, (2 * j + 1) % 8
                mmdeps = [ugk, ph2done] if n == 0 else []
                if j >= 4:
                    pj = j - 4
                    mmdeps.append(ev_ph3[(pj // NNC, pj % NNC)])
                    mmdeps.append(ev_ph3[(pj // NNC, pj % NNC, "s")])
                for k in range(KD):
                    P.t(lambda E, k=k, n=n, iu=iu, bau=bau: E.matmul(
                        pb[bau][:, :], ug[iu][:, k * 128:(k + 1) * 128],
                        lnres(k)[:, n * NCH:(n + 1) * NCH],
                        start=(k == 0), stop=(k == KD - 1)), deps=mmdeps if k == 0 else [])
                for k in range(KD):
                    mm_ph3[(ms, n)] = P.t(lambda E, k=k, n=n, iu=iu, bgp=bgp: E.matmul(
                        pb[bgp][:, :], ug[iu][:, (KD + k) * 128:(KD + k + 1) * 128],
                        lnres(k)[:, n * NCH:(n + 1) * NCH],
                        start=(k == 0), stop=(k == KD - 1)))
                mmj = mm_ph3[(ms, n)]
                vA = P.s(lambda E, bau=bau: E.copy(sAb, pb[bau][:, :]),
                         deps=[mmj, au_st.get((ms, n - 1)) if n >= 1
                               else au_st.get((ms - 1, NNC - 1))])
                au_st[(ms, n)] = P.dma("scalar", "ha", lambda E, ms=ms, n=n: E.dma_start(
                    out=HID[ms][:, n * NCH:(n + 1) * NCH], in_=sAb), deps=[vA])
                vB = P.v(lambda E, ms=ms, bgp=bgp: E.tensor_scalar(
                    sB, pb[bgp][:, :], sm_lpc[:, ms:ms + 1], None, ALU.add),
                    deps=[mmj] + ([s_nlp] if (ms == 0 and n == 0) else []))
                sab = P.s(lambda E: E.activation(sC, sB, AF.Abs), deps=[("ov", C["ov"])])
                P.v(lambda E: E.tensor_scalar(sC, sC, 2.5, None, ALU.is_le), deps=[sab])
                P.v(lambda E: E.tensor_mul(sD, sB, sB))
                P.v(lambda E: E.tensor_mul(sE, sD, sB))
                def tmpst(slot, dsrc, mi, ausrc, extra=()):
                    tdep = [(f"t{slot}", tst_cnt[slot] * 16)] if tst_cnt[slot] else []
                    vt = P.v(lambda E, dsrc=dsrc, mi=mi, ausrc=ausrc, slot=slot:
                             E.scalar_tensor_tensor(tstg[slot][:, 0:NCH], dsrc, sm_ccol[:, mi:mi + 1],
                                                    ausrc, ALU.mult, ALU.mult),
                             deps=list(extra) + tdep + ([ccol_fin] if (ms == 0 and n == 0 and mi == 0) else []))
                    tst_cnt[slot] += 1
                    return P.dma("sync", f"t{slot}", lambda E, mi=mi, ms=ms, n=n, slot=slot:
                                 E.dma_start(out=HID[8 + mi * 8 + ms][:, n * NCH:(n + 1) * NCH],
                                             in_=tstg[slot][:, 0:NCH]), deps=[vt])
                tmpst(0, sB, 0, pb[bau][:, :])
                vB2 = P.v(lambda E, bau=bau: E.tensor_mul(sB, sC, pb[bau][:, :]))
                tmpst(1, sD, 1, pb[bau][:, :])
                tmpst(2, sE, 2, pb[bau][:, :])
                P.v(lambda E: E.tensor_mul(sF, sD, sD))
                tmpst(3, sF, 3, pb[bau][:, :])
                P.v(lambda E: E.tensor_mul(sF, sD, sE))
                tmpst(4, sF, 4, sB, extra=[vB2])
                P.v(lambda E: E.tensor_mul(sF, sE, sE))
                tmpst(5, sF, 5, sB)
                ev_ph3[(ms, n)] = ("ov", C["ov"])
                ev_ph3[(ms, n, "s")] = ("os", C["os"])

        gb = [scrB(0, KD * 128), scrB(2048, KD * 128)]
        ub = [scrB(4096, KD * 128), scrB(6144, KD * 128)]
        sg = scr(8192, NCH)
        hstg = [scrB(8704, NCH), scrB(8960, NCH)]
        NF = FF // 128
        ph3_mm_done = P.now("mm")
        ph3_ov_done = ("ov", C["ov"])
        mm_ph4 = {}
        hv_ph4 = {}
        hst4 = {}
        for mf in range(NF):
            i = mf % 2
            wdeps = [ph3_ov_done, ph3_mm_done, ("ha", C["ha"])] if mf < 2 else [mm_ph4[(mf - 2, NNC - 1)]]
            gk = P.dma("gpsimd", f"fg{i}", lambda E, mf=mf, i=i: E.dma_start(
                out=gb[i][:, :], in_=FGp[mf]), deps=wdeps)
            uk = P.dma("gpsimd", f"fu{i}", lambda E, mf=mf, i=i: E.dma_start(
                out=ub[i][:, :], in_=FUp[mf]))
            for n in range(NNC):
                j = mf * NNC + n
                bg, bu = (2 * j) % 8, (2 * j + 1) % 8
                mmdeps = [gk, uk, ph3_mm_done] if n == 0 else []
                if j >= 4:
                    pj = j - 4
                    mmdeps.append(hv_ph4[(pj // NNC, pj % NNC)])
                for k in range(KD):
                    P.t(lambda E, k=k, n=n, i=i, bg=bg: E.matmul(
                        pb[bg][:, :], gb[i][:, k * 128:(k + 1) * 128],
                        lnres(k)[:, n * NCH:(n + 1) * NCH],
                        start=(k == 0), stop=(k == KD - 1)), deps=mmdeps if k == 0 else [])
                for k in range(KD):
                    mm_ph4[(mf, n)] = P.t(lambda E, k=k, n=n, i=i, bu=bu: E.matmul(
                        pb[bu][:, :], ub[i][:, k * 128:(k + 1) * 128],
                        lnres(k)[:, n * NCH:(n + 1) * NCH],
                        start=(k == 0), stop=(k == KD - 1)))
                mmj = mm_ph4[(mf, n)]
                ssil = P.s(lambda E, bg=bg: E.activation(sg, pb[bg][:, :], AF.Silu),
                           deps=[mmj, hv_ph4.get((mf, n - 1)) if n >= 1
                                 else hv_ph4.get((mf - 1, NNC - 1))])
                hv_ph4[(mf, n)] = P.v(lambda E, n=n, bu=bu: E.tensor_mul(
                    hstg[n % 2][:, :], sg, pb[bu][:, :]),
                    deps=[ssil, hst4.get((mf, n - 2)) or hst4.get((mf - 1, n + 2))])
                hst4[(mf, n)] = P.dma("sync", f"h{n % 2}", lambda E, mf=mf, n=n: E.dma_start(
                    out=HID[56 + mf][:, n * NCH:(n + 1) * NCH], in_=hstg[n % 2][:, :]),
                    deps=[hv_ph4[(mf, n)]])

        wb5 = [scrB(0, NWH * 128), scrB(4288, NWH * 128)]
        x1c = [scr(8576, NCH), scr(9088, NCH)]
        oc = [scr(9600, NCH), scr(10112, NCH)]
        ph4_mm_done = P.now("mm")
        ph4_st_done = [("h0", C["h0"]), ("h1", C["h1"]), ("ha", C["ha"])]
        ph3_t_done = [(f"t{s}", C[f"t{s}"]) for s in range(6)]
        mm_ph5 = {}
        ev_ph5 = {}
        st_ph5 = {}
        wld5 = {}
        for n in range(NNC):
            hdeps = ([ph4_mm_done] + ph4_st_done + ph3_t_done) if n == 0 else [mm_ph5[(n - 1, KD - 1)]]
            hchunk = {}
            for kk in range(NK_BIG):
                g = kk // 16
                P.dma("sync", f"hd{n % 2}c{g}", lambda E, kk=kk, n=n: E.dma_start(
                    out=hidres(kk), in_=HID[kk][:, n * NCH:(n + 1) * NCH]),
                    deps=hdeps if kk == 0 else [])
                hchunk[g] = P.now(f"hd{n % 2}c{g}")
            for mo in range(KD):
                jj5 = n * KD + mo
                wdeps = [ph4_mm_done, ("ov", C["ov"])] if jj5 == 0 else []
                wA = P.dma("gpsimd", "wbA", lambda E, mo=mo: E.dma_start(
                    out=wb5[0][:, :], in_=WB[mo][:, 0:NWH * 128]),
                    deps=wdeps + [("mmh0", C["mmh0"])])
                wB = P.dma("gpsimd", "wbB", lambda E, mo=mo: E.dma_start(
                    out=wb5[1][:, :], in_=WB[mo][:, NWH * 128:NK_BIG * 128]),
                    deps=wdeps + [("mmh1", C["mmh1"])])
                xk5 = P.dma("sync", f"y{mo % 2}", lambda E, mo=mo, n=n: E.dma_start(
                    out=x1c[mo % 2][:, :], in_=x1d[mo * 128:(mo + 1) * 128, n * NCH:(n + 1) * NCH]),
                    deps=[ev_ph5.get(((jj5 - 2) // KD, (jj5 - 2) % KD)) if jj5 >= 2 else None]
                         + ([ph4_mm_done, ("ov", C["ov"])] if jj5 < 2 else []))
                bank = pb[jj5 % 4 + 4]
                for kk in range(NK_BIG):
                    half = 0 if kk < NWH else 1
                    wdep = []
                    if mo == 0 and kk % 16 == 0:
                        wdep.append(hchunk[kk // 16])
                    if kk == 0:
                        wdep.append(wA)
                        if mo != 0:
                            wdep.append(hchunk[(NK_BIG - 1) // 16])
                        if jj5 >= 4:
                            q = jj5 - 4
                            wdep.append(ev_ph5[(q // KD, q % KD)])
                    elif kk == NWH:
                        wdep.append(wB)
                    inc = "mmh0" if kk == NWH - 1 else ("mmh1" if kk == NK_BIG - 1 else "mm")
                    mm_ph5[(n, mo)] = P.t(lambda E, kk=kk, half=half, bank=bank: E.matmul(
                        bank[:, :], wb5[half][:, (kk - half * NWH) * 128:(kk - half * NWH + 1) * 128],
                        hidres(kk), start=(kk == 0), stop=(kk == NK_BIG - 1)),
                        deps=wdep, inc=inc)
                pstq = jj5 - 2
                ev_ph5[(n, mo)] = P.v(lambda E, mo=mo, bank=bank: E.tensor_add(
                    oc[mo % 2][:, :], x1c[mo % 2][:, :], bank[:, :]),
                    deps=[mm_ph5[(n, mo)], xk5,
                          st_ph5.get((pstq // KD, pstq % KD)) if pstq >= 0 else None])
                st_ph5[(n, mo)] = P.dma("sync", f"z{mo % 2}", lambda E, mo=mo, n=n: E.dma_start(
                    out=x2o[mo * 128:(mo + 1) * 128, n * NCH:(n + 1) * NCH], in_=oc[mo % 2][:, :]),
                    deps=[ev_ph5[(n, mo)]])
        ph5_done = [P.now("z0"), P.now("z1"), P.now("mm"), P.now("mmh0"), P.now("mmh1"),
                    ("ov", C["ov"])]

        WAb6 = smla[0:128, 816:1456].bitcast(BF16)
        wald = P.dma("gpsimd", "wa6", lambda E: E.dma_start(out=WAb6, in_=WA[0]))
        wa6 = {
            "WAb": WAb6, "ld": (wald[0], C["wa6"]), "bAs": sm_bAs, "onesN": sm_onesN,
            "limTs": sm_limT, "ident": sm_ident,
            "zH": scrF[0:40, 0:NT], "h8s": scrF[0:8, NT:NT + NANCH],
            "dsls": scrF[0:8, NT + NANCH:2 * NT], "parts": sm_parts,
            "hm8": smla[0:8, 2938:2939], "dm8": smla[0:8, 2940:2941],
            "hm1": smla[0:1, 2988:2996], "dm1": smla[0:1, 2996:3004],
            "out_z": z3[:, :], "out_h8a": h8a3[:, :], "out_dsl": dsl3[:, :],
            "out_parts": partials3[:, :],
        }
        ph6 = emit_prepass(nc, P, pb, scrF, smla, x2o, lnto_big=lnres,
                           deps_start=ph5_done + [c_all],
                           tag="p6", sm_onesc=sm_onesc, sm_ones1r=sm_ones1r,
                           wa=wa6, store_lnx=lnx3, lnbig_free=ph5_done[2],
                           sm_onescb=sm_onescb)

        P.wait("sync", "p6o", C["p6o"])
        P.wait("sync", "p6s0", C["p6s0"])
        P.wait("sync", "p6s1", C["p6s1"])
        P.wait("sync", "z0", C["z0"])
        P.wait("sync", "z1", C["z1"])

        semnames = ["c", "cw", "cw2", "lnr", "g0", "w0", "w1", "x0", "x1",
                    "sx0", "sx1", "ov", "os", "mm", "p2x0", "p2x1",
                    "ug0", "ug1", "t0", "t1", "t2", "t3", "t4", "t5", "ha",
                    "fg0", "fg1", "fu0", "fu1",
                    "h0", "h1", "wbA", "wbB", "mmh0", "mmh1",
                    *[f"hd{pp}c{gg}" for pp in range(2) for gg in range(9)],
                    "y0", "y1", "z0", "z1", "wa6", "p6x0", "p6x1", "p6s0", "p6s1", "p6o"]
        _finalize(nc, P, es, semnames)
    return nc


def emit_prepass(nc, P, pb, scrF, smla, src_dram, lnto_big, deps_start, tag,
                 sm_onesc, sm_ones1r=None, wa=None, store_lnx=None, lnbig_free=None,
                 sm_onescb=None):
    C = P.C
    xb = [scrF[:, 0:NT], scrF[:, NT:2 * NT]]
    sq = [scrF[:, 2 * NT:3 * NT], scrF[:, 3 * NT:4 * NT]]
    sqb = [scrF[:, 2 * NT:2 * NT + NT // 2].bitcast(BF16),
           scrF[:, 3 * NT:3 * NT + NT // 2].bitcast(BF16)]
    Ba = scrF[:, 4 * NT:5 * NT]
    Bb = scrF[:, 5 * NT:6 * NT]
    r_s = scrF[0:1, 0:NT]
    r_sq = scrF[0:1, NT:2 * NT]
    r_a = scrF[0:1, 2 * NT:3 * NT]
    r_b = scrF[0:1, 3 * NT:4 * NT]
    vprev = None
    sqd = {}
    mmst = {}
    for k in range(KD):
        i = k % 2
        deps = list(deps_start) if k < 2 else []
        if k >= 2:
            deps.append(mmst[k - 2])
        qeng = "sync" if i == 0 else "scalar"
        xk = P.dma(qeng, f"{tag}x{i}", lambda E, k=k, i=i: E.dma_start(
            out=xb[i][:, :], in_=src_dram[k * 128:(k + 1) * 128, :]),
            deps=deps if k < 2 else deps)
        sqd[k] = P.s(lambda E, i=i: E.activation(sqb[i][:, :], xb[i][:, :], AF.Square),
                     deps=[xk] + ([mmst[k - 2]] if k >= 2 else []))
        for n in range(NNC):
            P.t(lambda E, k=k, n=n, i=i: E.matmul(
                pb[n][0:1, :], sm_onesc, xb[i][:, n * NCH:(n + 1) * NCH],
                start=(k == 0), stop=(k == KD - 1)),
                deps=[xk] if n == 0 else [])
        for n in range(NNC):
            mmst[k] = P.t(lambda E, k=k, n=n, i=i: E.matmul(
                pb[4 + n][0:1, :], sm_onescb, sqb[i][:, n * NCH:(n + 1) * NCH],
                start=(k == 0), stop=(k == KD - 1)),
                deps=[sqd[k]] if n == 0 else [])
    st_done = mmst[KD - 1]
    evs = []
    for n in range(NNC):
        evs.append(P.v(lambda E, n=n: E.tensor_copy(r_s[:, n * NCH:(n + 1) * NCH], pb[n][0:1, :]),
                       deps=[st_done] if n == 0 else []))
        evs.append(P.v(lambda E, n=n: E.tensor_copy(r_sq[:, n * NCH:(n + 1) * NCH], pb[4 + n][0:1, :])))
    P.v(lambda E: E.tensor_scalar_mul(r_s, r_s, 1.0 / DM))
    P.v(lambda E: E.tensor_scalar_mul(r_sq, r_sq, 1.0 / DM))
    P.v(lambda E: E.tensor_mul(r_a, r_s, r_s))
    P.v(lambda E: E.tensor_sub(r_sq, r_sq, r_a))
    veps = P.v(lambda E: E.tensor_scalar_add(r_sq, r_sq, 1e-5))
    sln = P.s(lambda E: E.activation(r_sq, r_sq, AF.Ln), deps=[veps])
    sexp = P.s(lambda E: E.activation(r_a, r_sq, AF.Exp, scale=-0.5), deps=[sln])
    vbr = P.v(lambda E: E.scalar_tensor_tensor(r_b, r_s, -1.0, r_a, ALU.mult, ALU.mult),
              deps=[sexp])
    prevb = {}
    jj = 0
    for (row, dst, dep) in ((r_a, Ba, sexp), (r_b, Bb, vbr)):
        for n in range(NNC):
            bank = pb[jj % 2]
            deps = [dep] + ([prevb[jj - 2]] if jj >= 2 else [evs[-1]])
            mmb = P.t(lambda E, row=row, n=n, bank=bank: E.matmul(
                bank[:, :], sm_ones1r, row[:, n * NCH:(n + 1) * NCH], start=True, stop=True),
                deps=deps)
            prevb[jj] = P.s(lambda E, dst=dst, n=n, bank=bank: E.copy(
                dst[:, n * NCH:(n + 1) * NCH], bank[:, :]), deps=[mmb])
            jj += 1
    bcf = prevb[jj - 1]
    vln = {}
    mm_wa = {}
    stl = {}
    for k in range(KD):
        i = k % 2
        bdeps = [vln[k - 2]] if k >= 2 else [st_done, bcf, prevb[jj - 1]]
        if wa is not None and k >= 2:
            bdeps.append(mm_wa[k - 2])
        if store_lnx is not None and k >= 2:
            bdeps.append(stl[k - 2])
        xk = P.dma("sync" if i == 0 else "scalar", f"{tag}x{i}", lambda E, k=k, i=i: E.dma_start(
            out=xb[i][:, :], in_=src_dram[k * 128:(k + 1) * 128, :]), deps=bdeps)
        P.v(lambda E, i=i: E.tensor_mul(sq[i][:, :], xb[i][:, :], Ba[:, :]),
            deps=[xk, bcf] + ([lnbig_free] if (k < 2 and lnbig_free is not None) else []))
        vln[k] = P.v(lambda E, k=k, i=i: E.tensor_add(lnto_big(k), sq[i][:, :], Bb[:, :]))
        if wa is not None:
            for n in range(NNC):
                mm_wa[k] = P.t(lambda E, k=k, n=n: E.matmul(
                    pb[4 + n][0:40, :], wa["WAb"][:, k * 40:(k + 1) * 40],
                    lnto_big(k)[:, n * NCH:(n + 1) * NCH],
                    start=(k == 0), stop=False),
                    deps=[vln[k], wa["ld"]] if n == 0 else [])
        if store_lnx is not None:
            stl[k] = P.dma("sync", f"{tag}s{i}", lambda E, k=k: E.dma_start(
                out=store_lnx[k * 128:(k + 1) * 128, :], in_=lnto_big(k)), deps=[vln[k]])
    res = {"done": vln[KD - 1], "vln": vln}
    if store_lnx is not None:
        res["st"] = [P.now(f"{tag}s0"), P.now(f"{tag}s1")]
    if wa is None:
        return res
    for n in range(NNC):
        P.t(lambda E, n=n: E.matmul(pb[4 + n][0:40, :], wa["bAs"], wa["onesN"],
                                    start=False, stop=True))
    mmdone = P.now("mm")
    zH, h8s, dsls, parts = wa["zH"], wa["h8s"], wa["dsls"], wa["parts"]
    ez = None
    for n in range(NNC):
        ez = P.v(lambda E, n=n: E.tensor_copy(zH[:, n * NCH:(n + 1) * NCH], pb[4 + n][0:40, :]),
                 deps=[mmdone] if n == 0 else [])
    vh8 = P.v(lambda E: E.tensor_copy(h8s[:, :], zH[32:40, 1::2]))
    dsl_ev = None
    for half in range(2):
        mmd = P.t(lambda E, half=half: E.matmul(
            pb[0][0:8, :], wa["limTs"], zH[0:32, 1 + half * 1024:min(1 + half * 1024 + 1024, NT):2],
            start=True, stop=True), deps=[ez] if half == 0 else [dsl_ev])
        dsl_ev = P.v(lambda E, half=half: E.tensor_copy(
            dsls[:, half * NCH:(half + 1) * NCH], pb[0][0:8, :]), deps=[mmd])
    hm8, dm8, hm1, dm1 = wa["hm8"], wa["dm8"], wa["hm1"], wa["dm1"]
    vm1 = P.v(lambda E: E.reduce_max(hm8, h8s[:, :], axis=AX.X, apply_absolute_value=True),
              deps=[vh8])
    vm2 = P.v(lambda E: E.reduce_max(dm8, dsls[:, :], axis=AX.X, apply_absolute_value=True),
              deps=[dsl_ev])
    mt1 = P.t(lambda E: E.matmul(pb[1][0:1, 0:8], hm8, wa["ident"][0:8, 0:8],
                                 start=True, stop=True), deps=[vm1])
    vt1 = P.v(lambda E: E.tensor_copy(hm1, pb[1][0:1, 0:8]), deps=[mt1])
    mt2 = P.t(lambda E: E.matmul(pb[2][0:1, 0:8], dm8, wa["ident"][0:8, 0:8],
                                 start=True, stop=True), deps=[vm2])
    vt2 = P.v(lambda E: E.tensor_copy(dm1, pb[2][0:1, 0:8]), deps=[mt2])
    P.v(lambda E: E.reduce_max(parts[:, 0:1], hm1, axis=AX.X), deps=[vt1])
    vparts = P.v(lambda E: E.reduce_max(parts[:, 1:2], dm1, axis=AX.X), deps=[vt2])
    P.dma("sync", f"{tag}o", lambda E: E.dma_start(out=wa["out_z"], in_=zH[0:32, :]), deps=[ez])
    P.dma("sync", f"{tag}o", lambda E: E.dma_start(out=wa["out_h8a"], in_=h8s[:, :]), deps=[vh8])
    P.dma("sync", f"{tag}o", lambda E: E.dma_start(out=wa["out_dsl"], in_=dsls[:, :]), deps=[dsl_ev])
    P.dma("sync", f"{tag}o", lambda E: E.dma_start(out=wa["out_parts"], in_=parts), deps=[vparts])
    res["out_st"] = P.now(f"{tag}o")
    return res


# version 04
# speedup vs baseline: 1.1089x; 1.1089x over previous
import contextlib
from collections import defaultdict

import numpy as np
import concourse.bass as bass
import concourse.mybir as mybir
from concourse.bass_utils import run_bass_kernel_spmd

F32 = mybir.dt.float32
BF16 = mybir.dt.bfloat16
AF = mybir.ActivationFunctionType
ALU = mybir.AluOpType
AX = mybir.AxisListType

NCORE = 8
DM = 4096
NT = 2048
NCH = 512
NNC = NT // NCH
KD = DM // 128
S = 1024
FF = 9984
G = 6
RANK = 32
E = 8
NANCH = NT // 2

ENGINES = ("sync", "tensor", "vector", "scalar", "gpsimd")


def _prep_lhsT(wT):
    K, M = wT.shape
    assert K % 128 == 0
    kt = K // 128
    mt = min(128, M)
    assert M % mt == 0
    mtiles = M // mt
    a = wT.reshape(kt, 128, mtiles, mt).transpose(2, 1, 0, 3)
    return np.ascontiguousarray(a).reshape(mtiles, 128, kt * mt)


class Plan:

    def __init__(self):
        self.q = {e: [] for e in ENGINES}
        self.C = defaultdict(int)
        self.sems = {}

    def raw(self, eng, fn):
        self.q[eng].append(fn)

    def wait(self, eng, semname, val):
        if val <= 0:
            return
        self.q[eng].append(lambda E, s=semname, v=val: E.wait_ge(self.sems[s], v))

    def op(self, eng, fn, deps=(), inc=None, inc_by=1):
        if inc is None and eng in ("vector", "scalar"):
            inc = {"vector": "ov", "scalar": "os"}[eng]
            self.wait(eng, inc, self.C[inc])
        for d in deps:
            if d is None:
                continue
            s, v = d
            self.wait(eng, s, v)
        if inc is not None:
            self.C[inc] += inc_by
            val = self.C[inc]
            self.q[eng].append(lambda E, fn=fn, s=inc, v=inc_by: fn(E).then_inc(self.sems[s], v))
            return (inc, val)
        self.q[eng].append(lambda E, fn=fn: fn(E))
        return None

    def v(self, fn, deps=()):
        return self.op("vector", fn, deps)

    def s(self, fn, deps=()):
        return self.op("scalar", fn, deps)

    def t(self, fn, deps=(), inc="mm"):
        return self.op("tensor", fn, deps, inc=inc)

    def dma(self, eng, semname, fn, deps=()):
        return self.op(eng, fn, deps, inc=semname, inc_by=16)

    def now(self, semname):
        return (semname, self.C[semname])


def _finalize(nc, plan, es, semnames):
    for n in semnames:
        plan.sems[n] = es.enter_context(nc.semaphore(n))
    with nc.Block() as block:
        sections = {"sync": block.sync, "tensor": block.tensor, "vector": block.vector,
                    "scalar": block.scalar, "gpsimd": block.gpsimd}
        for name in ENGINES:
            def body(E, name=name):
                for fn in plan.q[name]:
                    fn(E)
            sections[name](body)



def build_l1():
    nc = bass.Bass("TRN2", target_bir_lowering=False)
    x = nc.dram_tensor("x", [DM, NT], F32, kind="ExternalInput").ap()
    WA = nc.dram_tensor("WA", [1, 128, KD * 40], F32, kind="ExternalInput").ap()
    bA = nc.dram_tensor("bA", [1, 40], F32, kind="ExternalInput").ap()
    limT = nc.dram_tensor("limT", [RANK, E], F32, kind="ExternalInput").ap()
    onescol = nc.dram_tensor("onescol", [128, 1], F32, kind="ExternalInput").ap()
    identin = nc.dram_tensor("identin", [128, 128], F32, kind="ExternalInput").ap()
    lnx1 = nc.dram_tensor("lnx1", [DM, NT], BF16, kind="ExternalOutput").ap()
    z1 = nc.dram_tensor("z1", [RANK, NT], F32, kind="ExternalOutput").ap()
    h8a = nc.dram_tensor("h8a", [E, NANCH], F32, kind="ExternalOutput").ap()
    dsl = nc.dram_tensor("dsl", [E, NANCH], F32, kind="ExternalOutput").ap()
    partials = nc.dram_tensor("partials", [1, 2], F32, kind="ExternalOutput").ap()

    P = Plan()
    C = P.C
    es = contextlib.ExitStack()
    with es:
        def sbuf(name, shape, dt):
            return es.enter_context(nc.sbuf_tensor(name, shape, dt))
        def psum(name, shape, dt=F32):
            return es.enter_context(nc.psum_tensor(name, shape, dt))

        xb = [sbuf(f"xb{i}", [128, NT], F32) for i in range(2)]
        acc = sbuf("acc", [128, NT], F32)
        acc2 = sbuf("acc2", [128, NT], F32)
        sq = [sbuf(f"sq{i}", [128, NT], F32) for i in range(2)]
        onesc = sbuf("onesc", [128, 1], F32)
        ident = sbuf("ident", [128, 128], F32)
        ones1r = sbuf("ones1r", [1, 128], F32)
        onesN = sbuf("onesN", [1, NCH], F32)
        srow = sbuf("srow", [1, NT], F32)
        sqrow = sbuf("sqrow", [1, NT], F32)
        mean = sbuf("mean", [1, NT], F32)
        msq = sbuf("msq", [1, NT], F32)
        var = sbuf("var", [1, NT], F32)
        arow = sbuf("arow", [1, NT], F32)
        brow = sbuf("brow", [1, NT], F32)
        Ba = sbuf("Ba", [128, NT], F32)
        Bb = sbuf("Bb", [128, NT], F32)
        WAb = sbuf("WAb", [128, KD * 40], BF16)
        bAs = sbuf("bAs", [1, 40], F32)
        limTs = sbuf("limTs", [RANK, E], F32)
        lnxb = [sbuf(f"lnxb{i}", [128, NT], BF16) for i in range(2)]
        tb = [sbuf(f"tb{i}", [128, NT], F32) for i in range(2)]
        zH = sbuf("zH", [40, NT], F32)
        h8s = sbuf("h8s", [E, NANCH], F32)
        dsls = sbuf("dsls", [E, NANCH], F32)
        hm8 = sbuf("hm8", [E, 1], F32)
        dm8 = sbuf("dm8", [E, 1], F32)
        hm1 = sbuf("hm1", [1, E], F32)
        dm1 = sbuf("dm1", [1, E], F32)
        parts = sbuf("parts", [1, 2], F32)

        p_bc = [psum(f"p_bc{i}", [128, NCH]) for i in range(2)]
        p_wa = [psum(f"p_wa{i}", [40, NCH]) for i in range(4)]
        p_sl = psum("p_sl", [8, NCH])
        p_tr = psum("p_tr", [1, 8])

        for src, dst in ((onescol[:, :], onesc[:]), (identin[:, :], ident[:]),
                         (bA[:, :], bAs[:]), (limT[:, :], limTs[:])):
            P.dma("sync", "c", lambda E, s=src, d=dst: E.dma_start(out=d, in_=s))
        c_all = P.now("c")
        P.dma("gpsimd", "w", lambda E: E.dma_start(out=WAb[:], in_=WA[0]))
        w_all = P.now("w")
        m1 = P.v(lambda E: E.memset(ones1r[:], 1.0))
        m2 = P.v(lambda E: E.memset(onesN[:], 1.0))

        vacc = {}
        for k in range(KD):
            i = k % 2
            deps = [vacc[k - 2]] if k >= 2 else []
            xk = P.dma("sync", f"x{i}", lambda E, k=k, i=i: E.dma_start(
                out=xb[i][:], in_=x[k * 128:(k + 1) * 128, :]), deps=deps)
            sqk = P.s(lambda E, i=i: E.activation(sq[i][:], xb[i][:], AF.Square),
                      deps=[xk] + ([vacc[k - 2]] if k >= 2 else []))
            if k == 0:
                P.v(lambda E: E.tensor_copy(acc[:], xb[0][:]), deps=[xk])
                vacc[k] = P.v(lambda E: E.tensor_copy(acc2[:], sq[0][:]), deps=[sqk])
            else:
                P.v(lambda E, i=i: E.tensor_add(acc[:], acc[:], xb[i][:]), deps=[xk])
                vacc[k] = P.v(lambda E, i=i: E.tensor_add(acc2[:], acc2[:], sq[i][:]), deps=[sqk])

        st_copy = {}
        for j in range(2 * NNC):
            n, is_sq = j // 2, j % 2
            pb = p_bc[j % 2]
            srcacc = acc2 if is_sq else acc
            dstrow = sqrow if is_sq else srow
            deps = [c_all, vacc[KD - 1]] + ([st_copy[j - 2]] if j >= 2 else [])
            mmj = P.t(lambda E, n=n, pb=pb, a=srcacc: E.matmul(
                pb[0:1, :], onesc[:], a[:, n * NCH:(n + 1) * NCH], start=True, stop=True), deps=deps)
            st_copy[j] = P.v(lambda E, n=n, pb=pb, d=dstrow: E.tensor_copy(
                d[:, n * NCH:(n + 1) * NCH], pb[0:1, :]), deps=[mmj])
        last_st = st_copy[2 * NNC - 1]
        P.v(lambda E: E.tensor_scalar_mul(mean[:], srow[:], 1.0 / DM), deps=[last_st])
        P.v(lambda E: E.tensor_scalar_mul(var[:], sqrow[:], 1.0 / DM))
        P.v(lambda E: E.tensor_mul(msq[:], mean[:], mean[:]))
        P.v(lambda E: E.tensor_sub(var[:], var[:], msq[:]))
        veps = P.v(lambda E: E.tensor_scalar_add(var[:], var[:], 1e-5))
        sln = P.s(lambda E: E.activation(arow[:], var[:], AF.Ln), deps=[veps])
        sexp = P.s(lambda E: E.activation(arow[:], arow[:], AF.Exp, scale=-0.5), deps=[sln])
        vbrow = P.v(lambda E: E.scalar_tensor_tensor(brow[:], mean[:], -1.0, arow[:], ALU.mult, ALU.mult),
                    deps=[sexp])
        prev = {}
        jj = 0
        for (row, dst, dep) in ((arow, Ba, sexp), (brow, Bb, vbrow)):
            for n in range(NNC):
                pb = p_bc[jj % 2]
                deps = [dep, m1] + ([prev[jj - 2]] if jj >= 2 else [last_st])
                mmb = P.t(lambda E, row=row, n=n, pb=pb: E.matmul(
                    pb[:], ones1r[:], row[:, n * NCH:(n + 1) * NCH], start=True, stop=True), deps=deps)
                prev[jj] = P.s(lambda E, dst=dst, n=n, pb=pb: E.copy(
                    dst[:, n * NCH:(n + 1) * NCH], pb[:]), deps=[mmb])
                jj += 1
        bcast_fin = prev[jj - 1]

        vln, mm_wa, stln = {}, {}, {}
        for k in range(KD):
            i = k % 2
            deps = [vacc[KD - 1]] + ([vln[k - 2]] if k >= 2 else [])
            xk = P.dma("sync", f"x{i}", lambda E, k=k, i=i: E.dma_start(
                out=xb[i][:], in_=x[k * 128:(k + 1) * 128, :]), deps=deps)
            vdeps = [xk, bcast_fin]
            if k >= 2:
                vdeps += [mm_wa[k - 2], stln[k - 2]]
            P.v(lambda E, i=i: E.tensor_mul(tb[i][:], xb[i][:], Ba[:]), deps=vdeps)
            vln[k] = P.v(lambda E, i=i: E.tensor_add(lnxb[i][:], tb[i][:], Bb[:]))
            for n in range(NNC):
                mm_wa[k] = P.t(lambda E, k=k, n=n, i=i: E.matmul(
                    p_wa[n][:], WAb[:, k * 40:(k + 1) * 40], lnxb[i][:, n * NCH:(n + 1) * NCH],
                    start=(k == 0), stop=False),
                    deps=([vln[k], w_all] if n == 0 else []))
            stln[k] = P.dma("sync", f"st{i}", lambda E, k=k, i=i: E.dma_start(
                out=lnx1[k * 128:(k + 1) * 128, :], in_=lnxb[i][:]), deps=[vln[k]])
        for n in range(NNC):
            P.t(lambda E, n=n: E.matmul(p_wa[n][:], bAs[:], onesN[:], start=False, stop=True),
                deps=([c_all, m2] if n == 0 else []))
        mm_wa_done = P.now("mm")

        ez = {}
        for n in range(NNC):
            ez[n] = P.v(lambda E, n=n: E.tensor_copy(zH[:, n * NCH:(n + 1) * NCH], p_wa[n][:]),
                        deps=[mm_wa_done] if n == 0 else [])
        vh8 = P.v(lambda E: E.tensor_copy(h8s[:], zH[32:40, 1::2]))
        dsl_ev = {}
        for half in range(2):
            mmd = P.t(lambda E, half=half: E.matmul(
                p_sl[:], limTs[:], zH[0:32, 1 + half * 1024:min(1 + half * 1024 + 1024, NT):2],
                start=True, stop=True),
                deps=([ez[NNC - 1], c_all] if half == 0 else [dsl_ev[0]]))
            dsl_ev[half] = P.v(lambda E, half=half: E.tensor_copy(
                dsls[:, half * NCH:(half + 1) * NCH], p_sl[:]), deps=[mmd])
        vm1 = P.v(lambda E: E.reduce_max(hm8[:], h8s[:], axis=AX.X, apply_absolute_value=True),
                  deps=[vh8])
        vm2 = P.v(lambda E: E.reduce_max(dm8[:], dsls[:], axis=AX.X, apply_absolute_value=True),
                  deps=[dsl_ev[1]])
        mtr1 = P.t(lambda E: E.matmul(p_tr[:], hm8[:], ident[0:8, 0:8], start=True, stop=True),
                   deps=[vm1])
        vtr1 = P.v(lambda E: E.tensor_copy(hm1[:], p_tr[0:1, :]), deps=[mtr1])
        mtr2 = P.t(lambda E: E.matmul(p_tr[:], dm8[:], ident[0:8, 0:8], start=True, stop=True),
                   deps=[vm2, vtr1])
        vtr2 = P.v(lambda E: E.tensor_copy(dm1[:], p_tr[0:1, :]), deps=[mtr2])
        P.v(lambda E: E.reduce_max(parts[:, 0:1], hm1[:], axis=AX.X), deps=[vtr1])
        vparts = P.v(lambda E: E.reduce_max(parts[:, 1:2], dm1[:], axis=AX.X), deps=[vtr2])

        P.dma("sync", "so", lambda E: E.dma_start(out=z1[:, :], in_=zH[0:32, :]), deps=[ez[NNC - 1]])
        P.dma("sync", "so", lambda E: E.dma_start(out=h8a[:, :], in_=h8s[:]), deps=[vh8])
        P.dma("sync", "so", lambda E: E.dma_start(out=dsl[:, :], in_=dsls[:]), deps=[dsl_ev[1]])
        P.dma("sync", "so", lambda E: E.dma_start(out=partials[:, :], in_=parts[:]), deps=[vparts])
        P.wait("sync", "so", C["so"])
        P.wait("sync", "st0", C["st0"])
        P.wait("sync", "st1", C["st1"])

        semnames = ("c", "w", "x0", "x1", "st0", "st1", "so", "ov", "os", "mm")
        _finalize(nc, P, es, semnames)
    return nc



def _host_prep(inputs):
    p = {}
    x = np.asarray(inputs["L0_attn_hidden_states"])
    xt = np.ascontiguousarray(x.reshape(16384, DM).T)
    p["x_shards"] = [np.ascontiguousarray(xt[:, c * NT:(c + 1) * NT]) for c in range(NCORE)]

    W = np.asarray(inputs["L0_ffn_W"])
    A = np.asarray(inputs["L0_ffn_A"])
    b = np.asarray(inputs["L0_ffn_b"]).astype(np.float32)
    Wt = np.ascontiguousarray(W.T)
    WA = np.concatenate([A, Wt[:, 0:8]], axis=1)
    p["WA"] = _prep_lhsT(WA)
    p["bA"] = np.concatenate([np.zeros(32, np.float32), b[:8]]).reshape(1, 40)
    p["limT"] = np.ascontiguousarray(np.asarray(inputs["L0_ffn_LiMEs"]).T)
    p["onescol"] = np.ones((128, 1), np.float32)
    p["identin"] = np.eye(128, dtype=np.float32)
    p["Wp"] = _prep_lhsT(Wt)
    p["bcolp"] = np.ascontiguousarray(b.reshape(KD, 128).T)
    p["Bm"] = np.ascontiguousarray(np.asarray(inputs["L0_ffn_Bmat"]))
    p["lim"] = np.ascontiguousarray(np.asarray(inputs["L0_ffn_LiMEs"]))
    upP = _prep_lhsT(np.ascontiguousarray(np.asarray(inputs["L0_attn_up_w"]).T))
    gateP = _prep_lhsT(np.ascontiguousarray(np.asarray(inputs["L0_attn_gate_w"]).T))
    p["UGp"] = np.ascontiguousarray(np.concatenate([upP, gateP], axis=2))
    p["lpc"] = np.ascontiguousarray(np.asarray(inputs["L0_attn_local_point"]).reshape(E, 128).T)
    p["disc"] = np.asarray(inputs["L0_attn_discount_factor"]).reshape(1, G).astype(np.float32)
    p["FGp"] = _prep_lhsT(np.ascontiguousarray(np.asarray(inputs["L0_attn_ffn_gate_w"]).T))
    p["FUp"] = _prep_lhsT(np.ascontiguousarray(np.asarray(inputs["L0_attn_ffn_up_w"]).T))
    lao = np.asarray(inputs["L0_attn_local_approx_output"])
    fuse = np.asarray(inputs["L0_attn_fuse_weight"])
    parts = [_prep_lhsT(np.ascontiguousarray(lao.T))]
    for i in range(G):
        parts.append(_prep_lhsT(np.ascontiguousarray(fuse[:, :, i].T)))
    parts.append(_prep_lhsT(np.ascontiguousarray(np.asarray(inputs["L0_attn_ffn_down_w"]).T)))
    p["WB"] = np.ascontiguousarray(np.concatenate(parts, axis=2))
    return p


def kernel(**inputs):
    p = _host_prep(inputs)
    nc1 = build_l1()
    in_maps = [{"x": p["x_shards"][c], "WA": p["WA"], "bA": p["bA"],
                "limT": p["limT"], "onescol": p["onescol"], "identin": p["identin"]}
               for c in range(NCORE)]
    r1 = run_bass_kernel_spmd(nc1, in_maps, core_ids=list(range(NCORE)))
    kernel._l1 = r1
    pall = np.concatenate([r1.results[c]["partials"] for c in range(NCORE)], axis=0)
    pall = np.ascontiguousarray(pall)
    nc2 = build_l2()
    in_maps2 = [{
        "x": p["x_shards"][c], "lnx1": r1.results[c]["lnx1"], "z1": r1.results[c]["z1"],
        "h8a": r1.results[c]["h8a"], "dsl": r1.results[c]["dsl"], "pall": pall,
        "Wp": p["Wp"], "bcolp": p["bcolp"], "Bm": p["Bm"], "lim": p["lim"],
        "UGp": p["UGp"], "lpc": p["lpc"], "disc": p["disc"],
        "FGp": p["FGp"], "FUp": p["FUp"], "WB": p["WB"],
        "WA": p["WA"], "bA": p["bA"], "limT": p["limT"],
        "onescol": p["onescol"], "identin": p["identin"],
    } for c in range(NCORE)]
    r2 = run_bass_kernel_spmd(nc2, in_maps2, core_ids=list(range(NCORE)))
    kernel._l2 = r2
    pall3 = np.ascontiguousarray(
        np.concatenate([r2.results[c]["partials3"] for c in range(NCORE)], axis=0))
    nc3 = build_l2(mode="l3")
    in_maps3 = [{
        "x": r2.results[c]["x2"], "lnx1": r2.results[c]["lnx3"], "z1": r2.results[c]["z3"],
        "h8a": r2.results[c]["h8a3"], "dsl": r2.results[c]["dsl3"], "pall": pall3,
        "Wp": p["Wp"], "bcolp": p["bcolp"], "Bm": p["Bm"], "lim": p["lim"],
        "onescol": p["onescol"], "identin": p["identin"],
    } for c in range(NCORE)]
    r3 = run_bass_kernel_spmd(nc3, in_maps3, core_ids=list(range(NCORE)))
    kernel._l3 = r3
    full = np.concatenate([r3.results[c]["out"] for c in range(NCORE)], axis=1)
    return np.ascontiguousarray(full.T).reshape(2, 8192, DM)


NK_BIG = 134
DEBUG_L2 = False
NWH = 67
UGW = 64


def build_l2(phases=7, mode="l2"):
    nc = bass.Bass("TRN2", target_bir_lowering=False)
    dram = {}
    def din(name, shape, dt=F32):
        dram[name] = nc.dram_tensor(name, shape, dt, kind="ExternalInput").ap()
        return dram[name]
    def dout(name, shape, dt=F32):
        dram[name] = nc.dram_tensor(name, shape, dt, kind="ExternalOutput").ap()
        return dram[name]

    full = (mode == "l2")
    x = din("x", [DM, NT])
    lnx1 = din("lnx1", [DM, NT], BF16)
    z1 = din("z1", [RANK, NT])
    h8a = din("h8a", [E, NANCH])
    dsla = din("dsl", [E, NANCH])
    pall = din("pall", [E, 2])
    Wp = din("Wp", [KD, 128, KD * 128])
    bcolp = din("bcolp", [128, KD])
    Bm = din("Bm", [RANK, DM])
    lim = din("lim", [E, RANK])
    onescol = din("onescol", [128, 1])
    identin = din("identin", [128, 128])
    if full:
        UGp = din("UGp", [8, 128, KD * 2 * 128])
        lpc = din("lpc", [128, E])
        disc = din("disc", [1, G])
        FGp = din("FGp", [FF // 128, 128, KD * 128])
        FUp = din("FUp", [FF // 128, 128, KD * 128])
        WB = din("WB", [KD, 128, NK_BIG * 128])
        WA = din("WA", [1, 128, KD * 40])
        bA = din("bA", [1, 40])
        limT = din("limT", [RANK, E])
        x2o = dout("x2", [DM, NT])
        lnx3 = dout("lnx3", [DM, NT], BF16)
        z3 = dout("z3", [RANK, NT])
        h8a3 = dout("h8a3", [E, NANCH])
        dsl3 = dout("dsl3", [E, NANCH])
        partials3 = dout("partials3", [1, 2])
        x1d = nc.dram_tensor("x1d", [DM, NT], F32,
                             kind="ExternalOutput" if DEBUG_L2 else "Internal").ap()
        HID = nc.dram_tensor("HID", [NK_BIG, 128, NT], BF16, kind="Internal").ap()
    else:
        x1d = dout("out", [DM, NT])

    P = Plan()
    C = P.C
    es = contextlib.ExitStack()
    with es:
        big = es.enter_context(nc.sbuf_tensor("big", [128, 68608], BF16))
        scrF = es.enter_context(nc.sbuf_tensor("scrF", [128, 12288], F32))
        smla = es.enter_context(nc.sbuf_tensor("smla", [128, 3072], F32))
        pb = [es.enter_context(nc.psum_tensor(f"pb{i}", [128, NCH], F32)) for i in range(8)]

        def scr(c0, cols, parts=128, p0=0):
            return scrF[p0:p0 + parts, c0:c0 + cols]

        def scrB(c0, cols_bf16, parts=128, p0=0):
            assert cols_bf16 % 2 == 0
            return scrF[p0:p0 + parts, c0:c0 + cols_bf16 // 2].bitcast(BF16)

        def lnres(k):
            return big[:, k * NT:(k + 1) * NT]

        def hidres(kk):
            return big[:, kk * NCH:(kk + 1) * NCH]

        sm_ones1r = smla[0:1, 0:128]
        sm_onesN = smla[0:1, 128:640]
        sm_onesc = smla[0:128, 640:641]
        sm_ident = smla[0:128, 644:772]
        sm_hmax = smla[0:1, 772:780]
        sm_dmax = smla[0:1, 780:788]
        sm_hinv = smla[0:1, 788:789]
        sm_dinv = smla[0:1, 790:791]
        sm_s1c = smla[0:128, 792:793]
        sm_s2c = smla[0:128, 794:795]
        sm_ccol = smla[0:128, 796:802]
        sm_disc = smla[0:1, 802:808]
        sm_coef = smla[0:1, 808:814]
        sm_lp = smla[0:1, 816:1840]
        sm_zpb = smla[0:32, 1840:2864].bitcast(BF16)
        sm_limb = smla[0:8, 2864:2880].bitcast(BF16)
        sm_bAs = smla[0:1, 2880:2920]
        sm_limT = smla[0:32, 2920:2928]
        sm_parts = smla[0:1, 2930:2932]
        sm_pall = smla[0:8, 2932:2934]
        sm_pcol = smla[0:8, 2936:2937]
        sm_tm8 = smla[0:8, 2938:2939]
        sm_tm8b = smla[0:8, 2940:2941]
        sm_bcol = smla[0:128, 2944:2976]
        sm_onescb = smla[0:128, 2942:2943].bitcast(BF16)[:, 0:1]
        sm_lpc = smla[0:128, 2976:2984]

        def dma(eng, semname, fn, deps=()):
            return P.dma(eng, semname, fn, deps)

        cl = [(onescol[:, :], sm_onesc), (identin[:, :], sm_ident),
              (pall[:, :], sm_pall), (bcolp[:, :], sm_bcol)]
        if full:
            cl += [(bA[:, :], sm_bAs), (limT[:, :], sm_limT),
                   (disc[:, :], sm_disc), (lpc[:, :], sm_lpc)]
        for src, dst in cl:
            dma("sync", "c", lambda E, s=src, d=dst: E.dma_start(out=d, in_=s))
        c_all = P.now("c")
        dma("gpsimd", "cw", lambda E: E.dma_start(out=sm_limb, in_=lim[:, :]))
        cw_all = P.now("cw")
        m_ones = P.v(lambda E: E.memset(sm_ones1r, 1.0))
        m_ones2 = P.v(lambda E: E.memset(sm_onesN, 1.0))
        P.v(lambda E: E.memset(sm_onescb, 1.0))

        for k in range(KD):
            dma("sync", "lnr", lambda E, k=k: E.dma_start(
                out=lnres(k), in_=lnx1[k * 128:(k + 1) * 128, :]))
        lnr_all = P.now("lnr")

        g_z1s = scr(0, NT, 32)
        g_h8 = scr(2048, NANCH, 8)
        g_dsl = scr(3072, NANCH, 8)
        g_pmix = scr(4096, NANCH, 32)
        g_wT = scr(5120, NANCH, 8)
        g_wTb = scrB(6144, 2048, 8)
        gt = lambda j: scr(7168 + j * 16, 8)
        gc = lambda j: scr(7600 + j * 4, 1)

        ld_z1 = dma("sync", "g0", lambda E: E.dma_start(out=g_z1s, in_=z1[:, :]))
        ld_h8 = dma("sync", "g0", lambda E: E.dma_start(out=g_h8, in_=h8a[:, :]))
        ld_dsl = dma("sync", "g0", lambda E: E.dma_start(out=g_dsl, in_=dsla[:, :]))
        g0_all = P.now("g0")

        mt1 = P.t(lambda E: E.matmul(pb[0][0:1, 0:8], sm_pall[:, 0:1], sm_ident[0:8, 0:8],
                                     start=True, stop=True), deps=[c_all])
        vt1 = P.v(lambda E: E.tensor_copy(sm_hmax, pb[0][0:1, 0:8]), deps=[mt1])
        mt2 = P.t(lambda E: E.matmul(pb[1][0:1, 0:8], sm_pall[:, 1:2], sm_ident[0:8, 0:8],
                                     start=True, stop=True), deps=[vt1])
        vt2 = P.v(lambda E: E.tensor_copy(sm_dmax, pb[1][0:1, 0:8]), deps=[mt2])
        P.v(lambda E: E.reduce_max(sm_hinv, sm_hmax, axis=AX.X))
        P.v(lambda E: E.reduce_max(sm_dinv, sm_dmax, axis=AX.X))
        P.v(lambda E: E.tensor_scalar_max(sm_hinv, sm_hinv, 1e-6))
        P.v(lambda E: E.tensor_scalar_max(sm_dinv, sm_dinv, 1e-6))
        P.v(lambda E: E.reciprocal(sm_hinv, sm_hinv))
        P.v(lambda E: E.reciprocal(sm_dinv, sm_dinv))
        P.v(lambda E: E.tensor_scalar_mul(sm_hinv, sm_hinv, 0.5))
        vsc = P.v(lambda E: E.tensor_scalar_mul(sm_dinv, sm_dinv, 0.5))
        mb1 = P.t(lambda E: E.matmul(pb[2][:, 0:1], sm_ones1r, sm_hinv, start=True, stop=True),
                  deps=[vsc, m_ones])
        sb1 = P.s(lambda E: E.copy(sm_s1c, pb[2][:, 0:1]), deps=[mb1])
        mb2 = P.t(lambda E: E.matmul(pb[3][:, 0:1], sm_ones1r, sm_dinv, start=True, stop=True),
                  deps=[sb1])
        sb2 = P.s(lambda E: E.copy(sm_s2c, pb[3][:, 0:1]), deps=[mb2])
        scoef = P.s(lambda E: E.activation(sm_coef, sm_disc, AF.Exp, scale=-1.0), deps=[c_all]) if full else None
        ccps = []
        for m in range(G if full else 0):
            mc = P.t(lambda E, m=m: E.matmul(pb[4][:, 0:1], sm_ones1r, sm_coef[0:1, m:m + 1],
                                             start=True, stop=True),
                     deps=[scoef, m_ones] + ([ccps[-1]] if ccps else []))
            ccps.append(P.s(lambda E, m=m: E.copy(sm_ccol[:, m:m + 1], pb[4][:, 0:1]), deps=[mc]))
        ccol_fin = ccps[-1] if full else None
        s_nlp = P.s(lambda E: E.mul(sm_lpc, sm_lpc, -1.0), deps=[c_all]) if full else None

        wtr = []
        for cchunk in range(8):
            cs = slice(cchunk * 128, (cchunk + 1) * 128)
            mh = P.t(lambda E, cs=cs: E.matmul(pb[5][:, 0:8], g_h8[:, cs], sm_ident[0:8, 0:8],
                                               start=True, stop=True),
                     deps=[g0_all, c_all] + ([wtr[-1]] if wtr else []))
            vh = P.v(lambda E: E.tensor_copy(gt(0), pb[5][:, 0:8]), deps=[mh])
            md = P.t(lambda E, cs=cs: E.matmul(pb[6][:, 0:8], g_dsl[:, cs], sm_ident[0:8, 0:8],
                                               start=True, stop=True), deps=[vh])
            vd = P.v(lambda E: E.tensor_copy(gt(1), pb[6][:, 0:8]), deps=[md])
            P.v(lambda E: E.tensor_scalar_mul(gt(2), gt(1), sm_s2c[:, 0:1]), deps=[sb2])
            P.v(lambda E: E.scalar_tensor_tensor(gt(2), gt(0), sm_s1c[:, 0:1], gt(2),
                                                 ALU.mult, ALU.add))
            P.v(lambda E: E.reduce_max(gc(0), gt(2), axis=AX.X))
            vexp_in = P.v(lambda E: E.tensor_scalar(gt(3), gt(2), gc(0), None, ALU.subtract))
            sexp = P.s(lambda E: E.activation(gt(3), gt(3), AF.Exp), deps=[vexp_in])
            P.v(lambda E: E.reduce_sum(gc(1), gt(3), axis=AX.X), deps=[sexp])
            P.v(lambda E: E.reciprocal(gc(1), gc(1)))
            P.v(lambda E: E.tensor_scalar_mul(gt(4), gt(3), gc(1)))
            P.v(lambda E: E.reduce_max(gc(2), gt(4), axis=AX.X))
            P.v(lambda E: E.tensor_scalar(gt(5), gt(4), gc(2), None, ALU.is_ge))
            P.v(lambda E: E.reduce_sum(gc(3), gt(5), axis=AX.X))
            P.v(lambda E: E.scalar_tensor_tensor(gt(6), gt(5), -1e30, gt(4), ALU.mult, ALU.add))
            P.v(lambda E: E.reduce_max(gc(4), gt(6), axis=AX.X))
            P.v(lambda E: E.tensor_scalar(gc(5), gc(3), 2.0, None, ALU.is_ge))
            P.v(lambda E: E.tensor_sub(gc(6), gc(2), gc(4)))
            P.v(lambda E: E.tensor_mul(gc(6), gc(5), gc(6)))
            P.v(lambda E: E.tensor_add(gc(6), gc(6), gc(4)))
            vth = P.v(lambda E: E.tensor_scalar(gt(7), gt(4), gc(6), 2.0, ALU.subtract, ALU.mult))
            ssig = P.s(lambda E: E.activation(gt(7), gt(7), AF.Sigmoid), deps=[vth])
            P.v(lambda E: E.tensor_mul(gt(8), gt(4), gt(7)), deps=[ssig])
            P.v(lambda E: E.reduce_sum(gc(7), gt(8), axis=AX.X))
            P.v(lambda E: E.tensor_scalar_add(gc(7), gc(7), 1e-9))
            P.v(lambda E: E.reciprocal(gc(7), gc(7)))
            vw = P.v(lambda E: E.tensor_scalar_mul(gt(8), gt(8), gc(7)))
            mw = P.t(lambda E: E.matmul(pb[7][0:8, 0:128], gt(8), sm_ident[:, :],
                                        start=True, stop=True), deps=[vw])
            wtr.append(P.v(lambda E, cs=cs: E.tensor_copy(g_wT[:, cs], pb[7][0:8, 0:128]),
                           deps=[mw]))
        vwb = P.v(lambda E: E.tensor_copy(g_wTb[:, 0:NANCH], g_wT), deps=[wtr[-1]])
        pm_ev = []
        for half in range(2):
            mp = P.t(lambda E, half=half: E.matmul(
                pb[0][0:32, :], sm_limb, g_wTb[:, half * NCH:(half + 1) * NCH],
                start=True, stop=True), deps=[vwb, cw_all] + ([pm_ev[-1]] if pm_ev else []))
            pm_ev.append(P.v(lambda E, half=half: E.tensor_copy(
                g_pmix[:, half * NCH:(half + 1) * NCH], pb[0][0:32, :]), deps=[mp]))
        zpb_fin = None
        for par in range(2):
            zpb_fin = P.v(lambda E, par=par: E.scalar_tensor_tensor(
                sm_zpb[:, par::2], g_z1s[:, par::2], 0.5, g_pmix, ALU.mult, ALU.mult),
                deps=[pm_ev[-1], g0_all])

        p1_xres = [scr(0, NT), scr(2048, NT)]
        p1_obuf = scr(4096, NT)
        p1_wb = [scrB(6144, KD * 128), scrB(8192, KD * 128)]
        p1_Bmb = scrB(10240, DM, 32)

        dma("gpsimd", "cw2", lambda E: E.dma_start(out=p1_Bmb, in_=Bm[:, :]))
        cw2_all = P.now("cw2")

        ev_prev = {}
        st_prev = {}
        mmg = {}
        for m in range(KD):
            i = m % 2
            wdep = [mmg[(m - 2, NNC - 1)]] if m >= 2 else [zpb_fin]
            wk = dma("gpsimd", f"w{i}", lambda E, m=m, i=i: E.dma_start(
                out=p1_wb[i][:, :], in_=Wp[m]), deps=wdep)
            xk = dma("sync", f"x{i}", lambda E, m=m, i=i: E.dma_start(
                out=p1_xres[i][:, :], in_=x[m * 128:(m + 1) * 128, :]),
                deps=[ev_prev.get((m - 2, NNC - 1))] if m >= 2 else [zpb_fin])
            for n in range(NNC):
                bank = (m * NNC + n) % 8
                mmdeps = []
                if m == 0 and n == 0:
                    mmdeps = [lnr_all, wk, c_all, m_ones2]
                elif n == 0:
                    mmdeps = [wk]
                if (m * NNC + n) >= 8:
                    prev_grp = m * NNC + n - 8
                    mmdeps.append(ev_prev.get((prev_grp // NNC, prev_grp % NNC)))
                for k in range(KD):
                    P.t(lambda E, m=m, n=n, k=k, i=i, bank=bank: E.matmul(
                        pb[bank][:, :], p1_wb[i][:, k * 128:(k + 1) * 128],
                        lnres(k)[:, n * NCH:(n + 1) * NCH],
                        start=(k == 0), stop=False), deps=mmdeps if k == 0 else [])
                mmg[(m, n)] = P.t(lambda E, m=m, n=n, bank=bank: E.matmul(
                    pb[bank][:, :], p1_Bmb[:, m * 128:(m + 1) * 128],
                    sm_zpb[:, n * NCH:(n + 1) * NCH], start=False, stop=True),
                    deps=[zpb_fin, cw2_all] if (m == 0 and n == 0) else [])
                sev = P.v(lambda E, m=m, n=n, bank=bank: E.tensor_scalar(
                    p1_obuf[:, n * NCH:(n + 1) * NCH], pb[bank][:, :],
                    sm_bcol[:, m:m + 1], 0.5, ALU.add, ALU.mult),
                    deps=[mmg[(m, n)], c_all] + ([st_prev[m - 1]] if (n == 0 and m >= 1) else []))
                ev_prev[(m, n)] = P.v(lambda E, m=m, n=n, i=i: E.tensor_add(
                    p1_obuf[:, n * NCH:(n + 1) * NCH], p1_obuf[:, n * NCH:(n + 1) * NCH],
                    p1_xres[i][:, n * NCH:(n + 1) * NCH]),
                    deps=[xk])
            st_prev[m] = dma("sync", f"sx{m % 2}", lambda E, m=m: E.dma_start(
                out=x1d[m * 128:(m + 1) * 128, :], in_=p1_obuf[:, :]),
                deps=[ev_prev[(m, NNC - 1)]])
        ph1_done_st = [P.now("sx0"), P.now("sx1")]
        ph1_done_mm = P.now("mm")

        if phases <= 1 or not full:
            P.wait("sync", "sx0", C["sx0"])
            P.wait("sync", "sx1", C["sx1"])
            semnames = ["c", "cw", "cw2", "lnr", "g0", "w0", "w1", "x0", "x1",
                        "sx0", "sx1", "ov", "os", "mm"]
            _finalize(nc, P, es, semnames)
            return nc

        ph2 = emit_prepass(nc, P, pb, scrF, smla, x1d, lnto_big=lnres,
                           deps_start=[ph1_done_st[0], ph1_done_st[1], ph1_done_mm,
                                       ("ov", C["ov"]), c_all],
                           tag="p2", sm_onesc=sm_onesc, sm_ones1r=sm_ones1r,
                           sm_onescb=sm_onescb)

        ug = [scrB(0, KD * 2 * 128), scrB(4096, KD * 2 * 128)]
        sA = scr(8192, NCH)
        sAb = scrB(8192, NCH)
        sB = scr(8704, NCH)
        sC = scr(9216, NCH)
        sD = scr(9728, NCH)
        sE = scr(10240, NCH)
        sF = scr(10752, NCH)
        tstg = [scrB(11264, NCH), scrB(11520, NCH), None, None, None, None]
        ph2done = ph2["done"]
        tstg[2] = smla[0:128, 1840:2096].bitcast(BF16)
        tstg[3] = smla[0:128, 2096:2352].bitcast(BF16)
        tstg[4] = smla[0:128, 2352:2608].bitcast(BF16)
        tstg[5] = smla[0:128, 2608:2864].bitcast(BF16)
        mm_ph3 = {}
        ev_ph3 = {}
        tst_cnt = [0] * 6
        au_st = {}
        for ms in range(8):
            iu = ms % 2
            wdeps = [ph2done] if ms < 2 else [mm_ph3[(ms - 2, NNC - 1)]]
            ugk = P.dma("gpsimd", f"ug{iu}", lambda E, ms=ms, iu=iu: E.dma_start(
                out=ug[iu][:, :], in_=UGp[ms]), deps=wdeps)
            for n in range(NNC):
                j = ms * NNC + n
                bau, bgp = (2 * j) % 8, (2 * j + 1) % 8
                mmdeps = [ugk, ph2done] if n == 0 else []
                if j >= 4:
                    pj = j - 4
                    mmdeps.append(ev_ph3[(pj // NNC, pj % NNC)])
                    mmdeps.append(ev_ph3[(pj // NNC, pj % NNC, "s")])
                for k in range(KD):
                    P.t(lambda E, k=k, n=n, iu=iu, bau=bau: E.matmul(
                        pb[bau][:, :], ug[iu][:, k * 128:(k + 1) * 128],
                        lnres(k)[:, n * NCH:(n + 1) * NCH],
                        start=(k == 0), stop=(k == KD - 1)), deps=mmdeps if k == 0 else [])
                for k in range(KD):
                    mm_ph3[(ms, n)] = P.t(lambda E, k=k, n=n, iu=iu, bgp=bgp: E.matmul(
                        pb[bgp][:, :], ug[iu][:, (KD + k) * 128:(KD + k + 1) * 128],
                        lnres(k)[:, n * NCH:(n + 1) * NCH],
                        start=(k == 0), stop=(k == KD - 1)))
                mmj = mm_ph3[(ms, n)]
                vA = P.s(lambda E, bau=bau: E.copy(sAb, pb[bau][:, :]),
                         deps=[mmj, au_st.get((ms, n - 1)) if n >= 1
                               else au_st.get((ms - 1, NNC - 1))])
                au_st[(ms, n)] = P.dma("scalar", "ha", lambda E, ms=ms, n=n: E.dma_start(
                    out=HID[ms][:, n * NCH:(n + 1) * NCH], in_=sAb), deps=[vA])
                vB = P.v(lambda E, ms=ms, bgp=bgp: E.tensor_scalar(
                    sB, pb[bgp][:, :], sm_lpc[:, ms:ms + 1], None, ALU.add),
                    deps=[mmj] + ([s_nlp] if (ms == 0 and n == 0) else []))
                sab = P.s(lambda E: E.activation(sC, sB, AF.Abs), deps=[("ov", C["ov"])])
                P.v(lambda E: E.tensor_scalar(sC, sC, 2.5, None, ALU.is_le), deps=[sab])
                P.v(lambda E: E.tensor_mul(sD, sB, sB))
                P.v(lambda E: E.tensor_mul(sE, sD, sB))
                def tmpst(slot, dsrc, mi, ausrc, extra=()):
                    tdep = [(f"t{slot}", tst_cnt[slot] * 16)] if tst_cnt[slot] else []
                    vt = P.v(lambda E, dsrc=dsrc, mi=mi, ausrc=ausrc, slot=slot:
                             E.scalar_tensor_tensor(tstg[slot][:, 0:NCH], dsrc, sm_ccol[:, mi:mi + 1],
                                                    ausrc, ALU.mult, ALU.mult),
                             deps=list(extra) + tdep + ([ccol_fin] if (ms == 0 and n == 0 and mi == 0) else []))
                    tst_cnt[slot] += 1
                    return P.dma("sync", f"t{slot}", lambda E, mi=mi, ms=ms, n=n, slot=slot:
                                 E.dma_start(out=HID[8 + mi * 8 + ms][:, n * NCH:(n + 1) * NCH],
                                             in_=tstg[slot][:, 0:NCH]), deps=[vt])
                tmpst(0, sB, 0, pb[bau][:, :])
                vB2 = P.v(lambda E, bau=bau: E.tensor_mul(sB, sC, pb[bau][:, :]))
                tmpst(1, sD, 1, pb[bau][:, :])
                tmpst(2, sE, 2, pb[bau][:, :])
                P.v(lambda E: E.tensor_mul(sF, sD, sD))
                tmpst(3, sF, 3, pb[bau][:, :])
                P.v(lambda E: E.tensor_mul(sF, sD, sE))
                tmpst(4, sF, 4, sB, extra=[vB2])
                P.v(lambda E: E.tensor_mul(sF, sE, sE))
                tmpst(5, sF, 5, sB)
                ev_ph3[(ms, n)] = ("ov", C["ov"])
                ev_ph3[(ms, n, "s")] = ("os", C["os"])

        gb = [scrB(0, KD * 128), scrB(2048, KD * 128)]
        ub = [scrB(4096, KD * 128), scrB(6144, KD * 128)]
        sg = scr(8192, NCH)
        hstg = [scrB(8704, NCH), scrB(8960, NCH)]
        NF = FF // 128
        ph3_mm_done = P.now("mm")
        ph3_ov_done = ("ov", C["ov"])
        mm_ph4 = {}
        hv_ph4 = {}
        hst4 = {}
        for mf in range(NF):
            i = mf % 2
            wdeps = [ph3_ov_done, ph3_mm_done, ("ha", C["ha"])] if mf < 2 else [mm_ph4[(mf - 2, NNC - 1)]]
            gk = P.dma("gpsimd", f"fg{i}", lambda E, mf=mf, i=i: E.dma_start(
                out=gb[i][:, :], in_=FGp[mf]), deps=wdeps)
            uk = P.dma("gpsimd", f"fu{i}", lambda E, mf=mf, i=i: E.dma_start(
                out=ub[i][:, :], in_=FUp[mf]))
            for n in range(NNC):
                j = mf * NNC + n
                bg, bu = (2 * j) % 8, (2 * j + 1) % 8
                mmdeps = [gk, uk, ph3_mm_done] if n == 0 else []
                if j >= 4:
                    pj = j - 4
                    mmdeps.append(hv_ph4[(pj // NNC, pj % NNC)])
                for k in range(KD):
                    P.t(lambda E, k=k, n=n, i=i, bg=bg: E.matmul(
                        pb[bg][:, :], gb[i][:, k * 128:(k + 1) * 128],
                        lnres(k)[:, n * NCH:(n + 1) * NCH],
                        start=(k == 0), stop=(k == KD - 1)), deps=mmdeps if k == 0 else [])
                for k in range(KD):
                    mm_ph4[(mf, n)] = P.t(lambda E, k=k, n=n, i=i, bu=bu: E.matmul(
                        pb[bu][:, :], ub[i][:, k * 128:(k + 1) * 128],
                        lnres(k)[:, n * NCH:(n + 1) * NCH],
                        start=(k == 0), stop=(k == KD - 1)))
                mmj = mm_ph4[(mf, n)]
                ssil = P.s(lambda E, bg=bg: E.activation(sg, pb[bg][:, :], AF.Silu),
                           deps=[mmj, hv_ph4.get((mf, n - 1)) if n >= 1
                                 else hv_ph4.get((mf - 1, NNC - 1))])
                hv_ph4[(mf, n)] = P.v(lambda E, n=n, bu=bu: E.tensor_mul(
                    hstg[n % 2][:, :], sg, pb[bu][:, :]),
                    deps=[ssil, hst4.get((mf, n - 2)) or hst4.get((mf - 1, n + 2))])
                hst4[(mf, n)] = P.dma("sync", f"h{n % 2}", lambda E, mf=mf, n=n: E.dma_start(
                    out=HID[56 + mf][:, n * NCH:(n + 1) * NCH], in_=hstg[n % 2][:, :]),
                    deps=[hv_ph4[(mf, n)]])

        wb5 = [scrB(0, NWH * 128), scrB(4288, NWH * 128)]
        x1c = [scr(8576, NCH), scr(9088, NCH)]
        oc = [scr(9600, NCH), scr(10112, NCH)]
        ph4_mm_done = P.now("mm")
        ph4_st_done = [("h0", C["h0"]), ("h1", C["h1"]), ("ha", C["ha"])]
        ph3_t_done = [(f"t{s}", C[f"t{s}"]) for s in range(6)]
        mm_ph5 = {}
        ev_ph5 = {}
        st_ph5 = {}
        wld5 = {}
        for n in range(NNC):
            hdeps = ([ph4_mm_done] + ph4_st_done + ph3_t_done) if n == 0 else [mm_ph5[(n - 1, KD - 1)]]
            hchunk = {}
            for kk in range(NK_BIG):
                g = kk // 16
                P.dma("sync", f"hd{n % 2}c{g}", lambda E, kk=kk, n=n: E.dma_start(
                    out=hidres(kk), in_=HID[kk][:, n * NCH:(n + 1) * NCH]),
                    deps=hdeps if kk == 0 else [])
                hchunk[g] = P.now(f"hd{n % 2}c{g}")
            for mo in range(KD):
                jj5 = n * KD + mo
                wdeps = [ph4_mm_done, ("ov", C["ov"])] if jj5 == 0 else []
                wA = P.dma("gpsimd", "wbA", lambda E, mo=mo: E.dma_start(
                    out=wb5[0][:, :], in_=WB[mo][:, 0:NWH * 128]),
                    deps=wdeps + [("mmh0", C["mmh0"])])
                wB = P.dma("gpsimd", "wbB", lambda E, mo=mo: E.dma_start(
                    out=wb5[1][:, :], in_=WB[mo][:, NWH * 128:NK_BIG * 128]),
                    deps=wdeps + [("mmh1", C["mmh1"])])
                xk5 = P.dma("sync", f"y{mo % 2}", lambda E, mo=mo, n=n: E.dma_start(
                    out=x1c[mo % 2][:, :], in_=x1d[mo * 128:(mo + 1) * 128, n * NCH:(n + 1) * NCH]),
                    deps=[ev_ph5.get(((jj5 - 2) // KD, (jj5 - 2) % KD)) if jj5 >= 2 else None]
                         + ([ph4_mm_done, ("ov", C["ov"])] if jj5 < 2 else []))
                bank = pb[jj5 % 4 + 4]
                for kk in range(NK_BIG):
                    half = 0 if kk < NWH else 1
                    wdep = []
                    if mo == 0 and kk % 16 == 0:
                        wdep.append(hchunk[kk // 16])
                    if kk == 0:
                        wdep.append(wA)
                        if mo != 0:
                            wdep.append(hchunk[(NK_BIG - 1) // 16])
                        if jj5 >= 4:
                            q = jj5 - 4
                            wdep.append(ev_ph5[(q // KD, q % KD)])
                    elif kk == NWH:
                        wdep.append(wB)
                    inc = "mmh0" if kk == NWH - 1 else ("mmh1" if kk == NK_BIG - 1 else "mm")
                    mm_ph5[(n, mo)] = P.t(lambda E, kk=kk, half=half, bank=bank: E.matmul(
                        bank[:, :], wb5[half][:, (kk - half * NWH) * 128:(kk - half * NWH + 1) * 128],
                        hidres(kk), start=(kk == 0), stop=(kk == NK_BIG - 1)),
                        deps=wdep, inc=inc)
                pstq = jj5 - 2
                ev_ph5[(n, mo)] = P.v(lambda E, mo=mo, bank=bank: E.tensor_add(
                    oc[mo % 2][:, :], x1c[mo % 2][:, :], bank[:, :]),
                    deps=[mm_ph5[(n, mo)], xk5,
                          st_ph5.get((pstq // KD, pstq % KD)) if pstq >= 0 else None])
                st_ph5[(n, mo)] = P.dma("sync", f"z{mo % 2}", lambda E, mo=mo, n=n: E.dma_start(
                    out=x2o[mo * 128:(mo + 1) * 128, n * NCH:(n + 1) * NCH], in_=oc[mo % 2][:, :]),
                    deps=[ev_ph5[(n, mo)]])
        ph5_done = [P.now("z0"), P.now("z1"), P.now("mm"), P.now("mmh0"), P.now("mmh1"),
                    ("ov", C["ov"])]

        WAb6 = smla[0:128, 816:1456].bitcast(BF16)
        wald = P.dma("gpsimd", "wa6", lambda E: E.dma_start(out=WAb6, in_=WA[0]))
        wa6 = {
            "WAb": WAb6, "ld": (wald[0], C["wa6"]), "bAs": sm_bAs, "onesN": sm_onesN,
            "limTs": sm_limT, "ident": sm_ident,
            "zH": scrF[0:40, 0:NT], "h8s": scrF[0:8, NT:NT + NANCH],
            "dsls": scrF[0:8, NT + NANCH:2 * NT], "parts": sm_parts,
            "hm8": smla[0:8, 2938:2939], "dm8": smla[0:8, 2940:2941],
            "hm1": smla[0:1, 2988:2996], "dm1": smla[0:1, 2996:3004],
            "out_z": z3[:, :], "out_h8a": h8a3[:, :], "out_dsl": dsl3[:, :],
            "out_parts": partials3[:, :],
        }
        ph6 = emit_prepass(nc, P, pb, scrF, smla, x2o, lnto_big=lnres,
                           deps_start=ph5_done + [c_all],
                           tag="p6", sm_onesc=sm_onesc, sm_ones1r=sm_ones1r,
                           wa=wa6, store_lnx=lnx3, lnbig_free=ph5_done[2],
                           sm_onescb=sm_onescb)

        P.wait("sync", "p6o", C["p6o"])
        P.wait("sync", "p6s0", C["p6s0"])
        P.wait("sync", "p6s1", C["p6s1"])
        P.wait("sync", "z0", C["z0"])
        P.wait("sync", "z1", C["z1"])

        semnames = ["c", "cw", "cw2", "lnr", "g0", "w0", "w1", "x0", "x1",
                    "sx0", "sx1", "ov", "os", "mm", "p2x0", "p2x1",
                    "ug0", "ug1", "t0", "t1", "t2", "t3", "t4", "t5", "ha",
                    "fg0", "fg1", "fu0", "fu1",
                    "h0", "h1", "wbA", "wbB", "mmh0", "mmh1",
                    *[f"hd{pp}c{gg}" for pp in range(2) for gg in range(9)],
                    "y0", "y1", "z0", "z1", "wa6", "p6x0", "p6x1", "p6s0", "p6s1", "p6o"]
        _finalize(nc, P, es, semnames)
    return nc


def emit_prepass(nc, P, pb, scrF, smla, src_dram, lnto_big, deps_start, tag,
                 sm_onesc, sm_ones1r=None, wa=None, store_lnx=None, lnbig_free=None,
                 sm_onescb=None):
    C = P.C
    xb = [scrF[:, 0:NT], scrF[:, NT:2 * NT]]
    sq = [scrF[:, 2 * NT:3 * NT], scrF[:, 3 * NT:4 * NT]]
    sqb = [scrF[:, 2 * NT:2 * NT + NT // 2].bitcast(BF16),
           scrF[:, 3 * NT:3 * NT + NT // 2].bitcast(BF16)]
    Ba = scrF[:, 4 * NT:5 * NT]
    Bb = scrF[:, 5 * NT:6 * NT]
    r_s = scrF[0:1, 0:NT]
    r_sq = scrF[0:1, NT:2 * NT]
    r_a = scrF[0:1, 2 * NT:3 * NT]
    r_b = scrF[0:1, 3 * NT:4 * NT]
    vprev = None
    sqd = {}
    mmst = {}
    for k in range(KD):
        i = k % 2
        deps = list(deps_start) if k < 2 else []
        if k >= 2:
            deps.append(mmst[k - 2])
        xk = P.dma("sync", f"{tag}x{i}", lambda E, k=k, i=i: E.dma_start(
            out=xb[i][:, :], in_=src_dram[k * 128:(k + 1) * 128, :]),
            deps=deps if k < 2 else deps)
        sqd[k] = P.s(lambda E, i=i: E.activation(sqb[i][:, :], xb[i][:, :], AF.Square),
                     deps=[xk] + ([mmst[k - 2]] if k >= 2 else []))
        for n in range(NNC):
            P.t(lambda E, k=k, n=n, i=i: E.matmul(
                pb[n][0:1, :], sm_onesc, xb[i][:, n * NCH:(n + 1) * NCH],
                start=(k == 0), stop=(k == KD - 1)),
                deps=[xk] if n == 0 else [])
        for n in range(NNC):
            mmst[k] = P.t(lambda E, k=k, n=n, i=i: E.matmul(
                pb[4 + n][0:1, :], sm_onescb, sqb[i][:, n * NCH:(n + 1) * NCH],
                start=(k == 0), stop=(k == KD - 1)),
                deps=[sqd[k]] if n == 0 else [])
    st_done = mmst[KD - 1]
    evs = []
    for n in range(NNC):
        evs.append(P.v(lambda E, n=n: E.tensor_copy(r_s[:, n * NCH:(n + 1) * NCH], pb[n][0:1, :]),
                       deps=[st_done] if n == 0 else []))
        evs.append(P.v(lambda E, n=n: E.tensor_copy(r_sq[:, n * NCH:(n + 1) * NCH], pb[4 + n][0:1, :])))
    P.v(lambda E: E.tensor_scalar_mul(r_s, r_s, 1.0 / DM))
    P.v(lambda E: E.tensor_scalar_mul(r_sq, r_sq, 1.0 / DM))
    P.v(lambda E: E.tensor_mul(r_a, r_s, r_s))
    P.v(lambda E: E.tensor_sub(r_sq, r_sq, r_a))
    veps = P.v(lambda E: E.tensor_scalar_add(r_sq, r_sq, 1e-5))
    sln = P.s(lambda E: E.activation(r_sq, r_sq, AF.Ln), deps=[veps])
    sexp = P.s(lambda E: E.activation(r_a, r_sq, AF.Exp, scale=-0.5), deps=[sln])
    vbr = P.v(lambda E: E.scalar_tensor_tensor(r_b, r_s, -1.0, r_a, ALU.mult, ALU.mult),
              deps=[sexp])
    prevb = {}
    jj = 0
    for (row, dst, dep) in ((r_a, Ba, sexp), (r_b, Bb, vbr)):
        for n in range(NNC):
            bank = pb[jj % 2]
            deps = [dep] + ([prevb[jj - 2]] if jj >= 2 else [evs[-1]])
            mmb = P.t(lambda E, row=row, n=n, bank=bank: E.matmul(
                bank[:, :], sm_ones1r, row[:, n * NCH:(n + 1) * NCH], start=True, stop=True),
                deps=deps)
            prevb[jj] = P.s(lambda E, dst=dst, n=n, bank=bank: E.copy(
                dst[:, n * NCH:(n + 1) * NCH], bank[:, :]), deps=[mmb])
            jj += 1
    bcf = prevb[jj - 1]
    vln = {}
    mm_wa = {}
    stl = {}
    for k in range(KD):
        i = k % 2
        bdeps = [vln[k - 2]] if k >= 2 else [st_done, bcf, prevb[jj - 1]]
        if wa is not None and k >= 2:
            bdeps.append(mm_wa[k - 2])
        if store_lnx is not None and k >= 2:
            bdeps.append(stl[k - 2])
        xk = P.dma("sync", f"{tag}x{i}", lambda E, k=k, i=i: E.dma_start(
            out=xb[i][:, :], in_=src_dram[k * 128:(k + 1) * 128, :]), deps=bdeps)
        P.v(lambda E, i=i: E.tensor_mul(sq[i][:, :], xb[i][:, :], Ba[:, :]),
            deps=[xk, bcf] + ([lnbig_free] if (k < 2 and lnbig_free is not None) else []))
        vln[k] = P.v(lambda E, k=k, i=i: E.tensor_add(lnto_big(k), sq[i][:, :], Bb[:, :]))
        if wa is not None:
            for n in range(NNC):
                mm_wa[k] = P.t(lambda E, k=k, n=n: E.matmul(
                    pb[4 + n][0:40, :], wa["WAb"][:, k * 40:(k + 1) * 40],
                    lnto_big(k)[:, n * NCH:(n + 1) * NCH],
                    start=(k == 0), stop=False),
                    deps=[vln[k], wa["ld"]] if n == 0 else [])
        if store_lnx is not None:
            stl[k] = P.dma("sync", f"{tag}s{i}", lambda E, k=k: E.dma_start(
                out=store_lnx[k * 128:(k + 1) * 128, :], in_=lnto_big(k)), deps=[vln[k]])
    res = {"done": vln[KD - 1], "vln": vln}
    if store_lnx is not None:
        res["st"] = [P.now(f"{tag}s0"), P.now(f"{tag}s1")]
    if wa is None:
        return res
    for n in range(NNC):
        P.t(lambda E, n=n: E.matmul(pb[4 + n][0:40, :], wa["bAs"], wa["onesN"],
                                    start=False, stop=True))
    mmdone = P.now("mm")
    zH, h8s, dsls, parts = wa["zH"], wa["h8s"], wa["dsls"], wa["parts"]
    ez = None
    for n in range(NNC):
        ez = P.v(lambda E, n=n: E.tensor_copy(zH[:, n * NCH:(n + 1) * NCH], pb[4 + n][0:40, :]),
                 deps=[mmdone] if n == 0 else [])
    vh8 = P.v(lambda E: E.tensor_copy(h8s[:, :], zH[32:40, 1::2]))
    dsl_ev = None
    for half in range(2):
        mmd = P.t(lambda E, half=half: E.matmul(
            pb[0][0:8, :], wa["limTs"], zH[0:32, 1 + half * 1024:min(1 + half * 1024 + 1024, NT):2],
            start=True, stop=True), deps=[ez] if half == 0 else [dsl_ev])
        dsl_ev = P.v(lambda E, half=half: E.tensor_copy(
            dsls[:, half * NCH:(half + 1) * NCH], pb[0][0:8, :]), deps=[mmd])
    hm8, dm8, hm1, dm1 = wa["hm8"], wa["dm8"], wa["hm1"], wa["dm1"]
    vm1 = P.v(lambda E: E.reduce_max(hm8, h8s[:, :], axis=AX.X, apply_absolute_value=True),
              deps=[vh8])
    vm2 = P.v(lambda E: E.reduce_max(dm8, dsls[:, :], axis=AX.X, apply_absolute_value=True),
              deps=[dsl_ev])
    mt1 = P.t(lambda E: E.matmul(pb[1][0:1, 0:8], hm8, wa["ident"][0:8, 0:8],
                                 start=True, stop=True), deps=[vm1])
    vt1 = P.v(lambda E: E.tensor_copy(hm1, pb[1][0:1, 0:8]), deps=[mt1])
    mt2 = P.t(lambda E: E.matmul(pb[2][0:1, 0:8], dm8, wa["ident"][0:8, 0:8],
                                 start=True, stop=True), deps=[vm2])
    vt2 = P.v(lambda E: E.tensor_copy(dm1, pb[2][0:1, 0:8]), deps=[mt2])
    P.v(lambda E: E.reduce_max(parts[:, 0:1], hm1, axis=AX.X), deps=[vt1])
    vparts = P.v(lambda E: E.reduce_max(parts[:, 1:2], dm1, axis=AX.X), deps=[vt2])
    P.dma("sync", f"{tag}o", lambda E: E.dma_start(out=wa["out_z"], in_=zH[0:32, :]), deps=[ez])
    P.dma("sync", f"{tag}o", lambda E: E.dma_start(out=wa["out_h8a"], in_=h8s[:, :]), deps=[vh8])
    P.dma("sync", f"{tag}o", lambda E: E.dma_start(out=wa["out_dsl"], in_=dsls[:, :]), deps=[dsl_ev])
    P.dma("sync", f"{tag}o", lambda E: E.dma_start(out=wa["out_parts"], in_=parts), deps=[vparts])
    res["out_st"] = P.now(f"{tag}o")
    return res


# version 05
# speedup vs baseline: 1.1383x; 1.0265x over previous
import contextlib
from collections import defaultdict

import numpy as np
import concourse.bass as bass
import concourse.mybir as mybir
from concourse.bass_utils import run_bass_kernel_spmd

F32 = mybir.dt.float32
BF16 = mybir.dt.bfloat16
AF = mybir.ActivationFunctionType
ALU = mybir.AluOpType
AX = mybir.AxisListType

NCORE = 8
DM = 4096
NT = 2048
NCH = 512
NNC = NT // NCH
KD = DM // 128
S = 1024
FF = 9984
G = 6
RANK = 32
E = 8
NANCH = NT // 2

ENGINES = ("sync", "tensor", "vector", "scalar", "gpsimd")


def _prep_lhsT(wT):
    K, M = wT.shape
    assert K % 128 == 0
    kt = K // 128
    mt = min(128, M)
    assert M % mt == 0
    mtiles = M // mt
    a = wT.reshape(kt, 128, mtiles, mt).transpose(2, 1, 0, 3)
    return np.ascontiguousarray(a).reshape(mtiles, 128, kt * mt)


class Plan:

    def __init__(self):
        self.q = {e: [] for e in ENGINES}
        self.C = defaultdict(int)
        self.sems = {}

    def raw(self, eng, fn):
        self.q[eng].append(fn)

    def wait(self, eng, semname, val):
        if val <= 0:
            return
        self.q[eng].append(lambda E, s=semname, v=val: E.wait_ge(self.sems[s], v))

    def op(self, eng, fn, deps=(), inc=None, inc_by=1):
        if inc is None and eng in ("vector", "scalar"):
            inc = {"vector": "ov", "scalar": "os"}[eng]
            self.wait(eng, inc, self.C[inc])
        for d in deps:
            if d is None:
                continue
            s, v = d
            self.wait(eng, s, v)
        if inc is not None:
            self.C[inc] += inc_by
            val = self.C[inc]
            self.q[eng].append(lambda E, fn=fn, s=inc, v=inc_by: fn(E).then_inc(self.sems[s], v))
            return (inc, val)
        self.q[eng].append(lambda E, fn=fn: fn(E))
        return None

    def v(self, fn, deps=()):
        return self.op("vector", fn, deps)

    def s(self, fn, deps=()):
        return self.op("scalar", fn, deps)

    def t(self, fn, deps=(), inc="mm"):
        return self.op("tensor", fn, deps, inc=inc)

    def dma(self, eng, semname, fn, deps=()):
        return self.op(eng, fn, deps, inc=semname, inc_by=16)

    def now(self, semname):
        return (semname, self.C[semname])


def _finalize(nc, plan, es, semnames):
    for n in semnames:
        plan.sems[n] = es.enter_context(nc.semaphore(n))
    with nc.Block() as block:
        sections = {"sync": block.sync, "tensor": block.tensor, "vector": block.vector,
                    "scalar": block.scalar, "gpsimd": block.gpsimd}
        for name in ENGINES:
            def body(E, name=name):
                for fn in plan.q[name]:
                    fn(E)
            sections[name](body)



def build_l1():
    nc = bass.Bass("TRN2", target_bir_lowering=False)
    x = nc.dram_tensor("x", [DM, NT], F32, kind="ExternalInput").ap()
    WA = nc.dram_tensor("WA", [1, 128, KD * 40], F32, kind="ExternalInput").ap()
    bA = nc.dram_tensor("bA", [1, 40], F32, kind="ExternalInput").ap()
    limT = nc.dram_tensor("limT", [RANK, E], F32, kind="ExternalInput").ap()
    onescol = nc.dram_tensor("onescol", [128, 1], F32, kind="ExternalInput").ap()
    identin = nc.dram_tensor("identin", [128, 128], F32, kind="ExternalInput").ap()
    lnx1 = nc.dram_tensor("lnx1", [DM, NT], BF16, kind="ExternalOutput").ap()
    z1 = nc.dram_tensor("z1", [RANK, NT], F32, kind="ExternalOutput").ap()
    h8a = nc.dram_tensor("h8a", [E, NANCH], F32, kind="ExternalOutput").ap()
    dsl = nc.dram_tensor("dsl", [E, NANCH], F32, kind="ExternalOutput").ap()
    partials = nc.dram_tensor("partials", [1, 2], F32, kind="ExternalOutput").ap()

    P = Plan()
    C = P.C
    es = contextlib.ExitStack()
    with es:
        big = es.enter_context(nc.sbuf_tensor("big", [128, KD * NT], BF16))
        scrF = es.enter_context(nc.sbuf_tensor("scrF", [128, 12288], F32))
        smla = es.enter_context(nc.sbuf_tensor("smla", [128, 3072], F32))
        pb = [es.enter_context(nc.psum_tensor(f"pb{i}", [128, NCH], F32)) for i in range(8)]

        sm_ones1r = smla[0:1, 0:128]
        sm_onesN = smla[0:1, 128:640]
        sm_onesc = smla[0:128, 640:641]
        sm_ident = smla[0:128, 644:772]
        sm_bAs = smla[0:1, 2880:2920]
        sm_limT = smla[0:32, 2920:2928]
        sm_parts = smla[0:1, 2930:2932]
        sm_onescb = smla[0:128, 2942:2943].bitcast(BF16)[:, 0:1]
        WAb = smla[0:128, 816:1456].bitcast(BF16)

        for s_, d_ in ((onescol[:, :], sm_onesc), (identin[:, :], sm_ident),
                       (bA[:, :], sm_bAs), (limT[:, :], sm_limT)):
            P.dma("sync", "c", lambda E, s=s_, d=d_: E.dma_start(out=d, in_=s))
        c_all = P.now("c")
        wald = P.dma("gpsimd", "wa6", lambda E: E.dma_start(out=WAb, in_=WA[0]))
        P.v(lambda E: E.memset(sm_ones1r, 1.0))
        P.v(lambda E: E.memset(sm_onesN, 1.0))
        P.v(lambda E: E.memset(sm_onescb, 1.0))
        m_ones = ("ov", C["ov"])

        wa = {
            "WAb": WAb, "ld": wald, "bAs": sm_bAs, "onesN": sm_onesN,
            "limTs": sm_limT, "ident": sm_ident,
            "zH": scrF[0:40, 0:NT], "h8s": scrF[0:8, NT:NT + NANCH],
            "dsls": scrF[0:8, NT + NANCH:2 * NT], "parts": sm_parts,
            "hm8": smla[0:8, 2938:2939], "dm8": smla[0:8, 2940:2941],
            "hm1": smla[0:1, 2988:2996], "dm1": smla[0:1, 2996:3004],
            "out_z": z1[:, :], "out_h8a": h8a[:, :], "out_dsl": dsl[:, :],
            "out_parts": partials[:, :],
        }
        emit_prepass(nc, P, pb, scrF, smla, x,
                     lnto_big=lambda k: big[:, k * NT:(k + 1) * NT],
                     deps_start=[c_all, m_ones], tag="l1",
                     sm_onesc=sm_onesc, sm_ones1r=sm_ones1r, wa=wa,
                     store_lnx=lnx1, sm_onescb=sm_onescb)
        P.wait("sync", "l1o", C["l1o"])
        P.wait("sync", "l1s0", C["l1s0"])
        P.wait("sync", "l1s1", C["l1s1"])
        semnames = ["c", "wa6", "l1x0", "l1x1", "l1s0", "l1s1", "l1o", "ov", "os", "mm"]
        _finalize(nc, P, es, semnames)
    return nc



def _host_prep(inputs):
    p = {}
    x = np.asarray(inputs["L0_attn_hidden_states"])
    xt = np.ascontiguousarray(x.reshape(16384, DM).T)
    p["x_shards"] = [np.ascontiguousarray(xt[:, c * NT:(c + 1) * NT]) for c in range(NCORE)]

    W = np.asarray(inputs["L0_ffn_W"])
    A = np.asarray(inputs["L0_ffn_A"])
    b = np.asarray(inputs["L0_ffn_b"]).astype(np.float32)
    Wt = np.ascontiguousarray(W.T)
    WA = np.concatenate([A, Wt[:, 0:8]], axis=1)
    p["WA"] = _prep_lhsT(WA)
    p["bA"] = np.concatenate([np.zeros(32, np.float32), b[:8]]).reshape(1, 40)
    p["limT"] = np.ascontiguousarray(np.asarray(inputs["L0_ffn_LiMEs"]).T)
    p["onescol"] = np.ones((128, 1), np.float32)
    p["identin"] = np.eye(128, dtype=np.float32)
    p["Wp"] = _prep_lhsT(Wt)
    p["bcolp"] = np.ascontiguousarray(b.reshape(KD, 128).T)
    p["Bm"] = np.ascontiguousarray(np.asarray(inputs["L0_ffn_Bmat"]))
    p["lim"] = np.ascontiguousarray(np.asarray(inputs["L0_ffn_LiMEs"]))
    upP = _prep_lhsT(np.ascontiguousarray(np.asarray(inputs["L0_attn_up_w"]).T))
    gateP = _prep_lhsT(np.ascontiguousarray(np.asarray(inputs["L0_attn_gate_w"]).T))
    p["UGp"] = np.ascontiguousarray(np.concatenate([upP, gateP], axis=2))
    p["lpc"] = np.ascontiguousarray(np.asarray(inputs["L0_attn_local_point"]).reshape(E, 128).T)
    p["disc"] = np.asarray(inputs["L0_attn_discount_factor"]).reshape(1, G).astype(np.float32)
    p["FGp"] = _prep_lhsT(np.ascontiguousarray(np.asarray(inputs["L0_attn_ffn_gate_w"]).T))
    p["FUp"] = _prep_lhsT(np.ascontiguousarray(np.asarray(inputs["L0_attn_ffn_up_w"]).T))
    lao = np.asarray(inputs["L0_attn_local_approx_output"])
    fuse = np.asarray(inputs["L0_attn_fuse_weight"])
    parts = [_prep_lhsT(np.ascontiguousarray(lao.T))]
    for i in range(G):
        parts.append(_prep_lhsT(np.ascontiguousarray(fuse[:, :, i].T)))
    parts.append(_prep_lhsT(np.ascontiguousarray(np.asarray(inputs["L0_attn_ffn_down_w"]).T)))
    p["WB"] = np.ascontiguousarray(np.concatenate(parts, axis=2))
    return p


def kernel(**inputs):
    p = _host_prep(inputs)
    nc1 = build_l1()
    in_maps = [{"x": p["x_shards"][c], "WA": p["WA"], "bA": p["bA"],
                "limT": p["limT"], "onescol": p["onescol"], "identin": p["identin"]}
               for c in range(NCORE)]
    r1 = run_bass_kernel_spmd(nc1, in_maps, core_ids=list(range(NCORE)))
    kernel._l1 = r1
    pall = np.concatenate([r1.results[c]["partials"] for c in range(NCORE)], axis=0)
    pall = np.ascontiguousarray(pall)
    nc2 = build_l2()
    in_maps2 = [{
        "x": p["x_shards"][c], "lnx1": r1.results[c]["lnx1"], "z1": r1.results[c]["z1"],
        "h8a": r1.results[c]["h8a"], "dsl": r1.results[c]["dsl"], "pall": pall,
        "Wp": p["Wp"], "bcolp": p["bcolp"], "Bm": p["Bm"], "lim": p["lim"],
        "UGp": p["UGp"], "lpc": p["lpc"], "disc": p["disc"],
        "FGp": p["FGp"], "FUp": p["FUp"], "WB": p["WB"],
        "WA": p["WA"], "bA": p["bA"], "limT": p["limT"],
        "onescol": p["onescol"], "identin": p["identin"],
    } for c in range(NCORE)]
    r2 = run_bass_kernel_spmd(nc2, in_maps2, core_ids=list(range(NCORE)))
    kernel._l2 = r2
    pall3 = np.ascontiguousarray(
        np.concatenate([r2.results[c]["partials3"] for c in range(NCORE)], axis=0))
    nc3 = build_l2(mode="l3")
    in_maps3 = [{
        "x": r2.results[c]["x2"], "lnx1": r2.results[c]["lnx3"], "z1": r2.results[c]["z3"],
        "h8a": r2.results[c]["h8a3"], "dsl": r2.results[c]["dsl3"], "pall": pall3,
        "Wp": p["Wp"], "bcolp": p["bcolp"], "Bm": p["Bm"], "lim": p["lim"],
        "onescol": p["onescol"], "identin": p["identin"],
    } for c in range(NCORE)]
    r3 = run_bass_kernel_spmd(nc3, in_maps3, core_ids=list(range(NCORE)))
    kernel._l3 = r3
    full = np.concatenate([r3.results[c]["out"] for c in range(NCORE)], axis=1)
    return np.ascontiguousarray(full.T).reshape(2, 8192, DM)


NK_BIG = 134
DEBUG_L2 = False
NWH = 67
UGW = 64


def build_l2(phases=7, mode="l2"):
    nc = bass.Bass("TRN2", target_bir_lowering=False)
    dram = {}
    def din(name, shape, dt=F32):
        dram[name] = nc.dram_tensor(name, shape, dt, kind="ExternalInput").ap()
        return dram[name]
    def dout(name, shape, dt=F32):
        dram[name] = nc.dram_tensor(name, shape, dt, kind="ExternalOutput").ap()
        return dram[name]

    full = (mode == "l2")
    x = din("x", [DM, NT])
    lnx1 = din("lnx1", [DM, NT], BF16)
    z1 = din("z1", [RANK, NT])
    h8a = din("h8a", [E, NANCH])
    dsla = din("dsl", [E, NANCH])
    pall = din("pall", [E, 2])
    Wp = din("Wp", [KD, 128, KD * 128])
    bcolp = din("bcolp", [128, KD])
    Bm = din("Bm", [RANK, DM])
    lim = din("lim", [E, RANK])
    onescol = din("onescol", [128, 1])
    identin = din("identin", [128, 128])
    if full:
        UGp = din("UGp", [8, 128, KD * 2 * 128])
        lpc = din("lpc", [128, E])
        disc = din("disc", [1, G])
        FGp = din("FGp", [FF // 128, 128, KD * 128])
        FUp = din("FUp", [FF // 128, 128, KD * 128])
        WB = din("WB", [KD, 128, NK_BIG * 128])
        WA = din("WA", [1, 128, KD * 40])
        bA = din("bA", [1, 40])
        limT = din("limT", [RANK, E])
        x2o = dout("x2", [DM, NT])
        lnx3 = dout("lnx3", [DM, NT], BF16)
        z3 = dout("z3", [RANK, NT])
        h8a3 = dout("h8a3", [E, NANCH])
        dsl3 = dout("dsl3", [E, NANCH])
        partials3 = dout("partials3", [1, 2])
        x1d = nc.dram_tensor("x1d", [DM, NT], F32,
                             kind="ExternalOutput" if DEBUG_L2 else "Internal").ap()
        HID = nc.dram_tensor("HID", [NK_BIG, 128, NT], BF16, kind="Internal").ap()
    else:
        x1d = dout("out", [DM, NT])

    P = Plan()
    C = P.C
    es = contextlib.ExitStack()
    with es:
        big = es.enter_context(nc.sbuf_tensor("big", [128, 68608], BF16))
        scrF = es.enter_context(nc.sbuf_tensor("scrF", [128, 12288], F32))
        smla = es.enter_context(nc.sbuf_tensor("smla", [128, 3072], F32))
        pb = [es.enter_context(nc.psum_tensor(f"pb{i}", [128, NCH], F32)) for i in range(8)]

        def scr(c0, cols, parts=128, p0=0):
            return scrF[p0:p0 + parts, c0:c0 + cols]

        def scrB(c0, cols_bf16, parts=128, p0=0):
            assert cols_bf16 % 2 == 0
            return scrF[p0:p0 + parts, c0:c0 + cols_bf16 // 2].bitcast(BF16)

        def lnres(k):
            return big[:, k * NT:(k + 1) * NT]

        def hidres(kk):
            return big[:, kk * NCH:(kk + 1) * NCH]

        sm_ones1r = smla[0:1, 0:128]
        sm_onesN = smla[0:1, 128:640]
        sm_onesc = smla[0:128, 640:641]
        sm_ident = smla[0:128, 644:772]
        sm_hmax = smla[0:1, 772:780]
        sm_dmax = smla[0:1, 780:788]
        sm_hinv = smla[0:1, 788:789]
        sm_dinv = smla[0:1, 790:791]
        sm_s1c = smla[0:128, 792:793]
        sm_s2c = smla[0:128, 794:795]
        sm_ccol = smla[0:128, 796:802]
        sm_disc = smla[0:1, 802:808]
        sm_coef = smla[0:1, 808:814]
        sm_lp = smla[0:1, 816:1840]
        sm_zpb = smla[0:32, 1840:2864].bitcast(BF16)
        sm_limb = smla[0:8, 2864:2880].bitcast(BF16)
        sm_bAs = smla[0:1, 2880:2920]
        sm_limT = smla[0:32, 2920:2928]
        sm_parts = smla[0:1, 2930:2932]
        sm_pall = smla[0:8, 2932:2934]
        sm_pcol = smla[0:8, 2936:2937]
        sm_tm8 = smla[0:8, 2938:2939]
        sm_tm8b = smla[0:8, 2940:2941]
        sm_bcol = smla[0:128, 2944:2976]
        sm_onescb = smla[0:128, 2942:2943].bitcast(BF16)[:, 0:1]
        sm_lpc = smla[0:128, 2976:2984]

        def dma(eng, semname, fn, deps=()):
            return P.dma(eng, semname, fn, deps)

        cl = [(onescol[:, :], sm_onesc), (identin[:, :], sm_ident),
              (pall[:, :], sm_pall), (bcolp[:, :], sm_bcol)]
        if full:
            cl += [(bA[:, :], sm_bAs), (limT[:, :], sm_limT),
                   (disc[:, :], sm_disc), (lpc[:, :], sm_lpc)]
        for src, dst in cl:
            dma("sync", "c", lambda E, s=src, d=dst: E.dma_start(out=d, in_=s))
        c_all = P.now("c")
        dma("gpsimd", "cw", lambda E: E.dma_start(out=sm_limb, in_=lim[:, :]))
        cw_all = P.now("cw")
        m_ones = P.v(lambda E: E.memset(sm_ones1r, 1.0))
        m_ones2 = P.v(lambda E: E.memset(sm_onesN, 1.0))
        P.v(lambda E: E.memset(sm_onescb, 1.0))

        for k in range(KD):
            dma("sync" if k % 2 == 0 else "gpsimd", f"lnr{k % 2}", lambda E, k=k: E.dma_start(
                out=lnres(k), in_=lnx1[k * 128:(k + 1) * 128, :]))
        lnr_all = [P.now("lnr0"), P.now("lnr1")]

        g_z1s = scr(0, NT, 32)
        g_h8 = scr(2048, NANCH, 8)
        g_dsl = scr(3072, NANCH, 8)
        g_pmix = scr(4096, NANCH, 32)
        g_wT = scr(5120, NANCH, 8)
        g_wTb = scrB(10240, 2048, 8)
        gt = lambda j: scr(11264 + j * 16, 8)
        gc = lambda j: scr(11600 + j * 4, 1)

        ld_z1 = dma("sync", "g0", lambda E: E.dma_start(out=g_z1s, in_=z1[:, :]))
        ld_h8 = dma("sync", "g0", lambda E: E.dma_start(out=g_h8, in_=h8a[:, :]))
        ld_dsl = dma("sync", "g0", lambda E: E.dma_start(out=g_dsl, in_=dsla[:, :]))
        g0_all = P.now("g0")

        mt1 = P.t(lambda E: E.matmul(pb[6][0:1, 0:8], sm_pall[:, 0:1], sm_ident[0:8, 0:8],
                                     start=True, stop=True), deps=[c_all])
        vt1 = P.v(lambda E: E.tensor_copy(sm_hmax, pb[6][0:1, 0:8]), deps=[mt1])
        mt2 = P.t(lambda E: E.matmul(pb[7][0:1, 0:8], sm_pall[:, 1:2], sm_ident[0:8, 0:8],
                                     start=True, stop=True), deps=[vt1])
        vt2 = P.v(lambda E: E.tensor_copy(sm_dmax, pb[7][0:1, 0:8]), deps=[mt2])
        P.v(lambda E: E.reduce_max(sm_hinv, sm_hmax, axis=AX.X))
        P.v(lambda E: E.reduce_max(sm_dinv, sm_dmax, axis=AX.X))
        P.v(lambda E: E.tensor_scalar_max(sm_hinv, sm_hinv, 1e-6))
        P.v(lambda E: E.tensor_scalar_max(sm_dinv, sm_dinv, 1e-6))
        P.v(lambda E: E.reciprocal(sm_hinv, sm_hinv))
        P.v(lambda E: E.reciprocal(sm_dinv, sm_dinv))
        P.v(lambda E: E.tensor_scalar_mul(sm_hinv, sm_hinv, 0.5))
        vsc = P.v(lambda E: E.tensor_scalar_mul(sm_dinv, sm_dinv, 0.5))
        mb1 = P.t(lambda E: E.matmul(pb[6][:, 0:1], sm_ones1r, sm_hinv, start=True, stop=True),
                  deps=[vsc, m_ones])
        sb1 = P.s(lambda E: E.copy(sm_s1c, pb[6][:, 0:1]), deps=[mb1])
        mb2 = P.t(lambda E: E.matmul(pb[7][:, 0:1], sm_ones1r, sm_dinv, start=True, stop=True),
                  deps=[sb1])
        sb2 = P.s(lambda E: E.copy(sm_s2c, pb[7][:, 0:1]), deps=[mb2])
        scoef = P.s(lambda E: E.activation(sm_coef, sm_disc, AF.Exp, scale=-1.0), deps=[c_all]) if full else None
        ccps = []
        for m in range(G if full else 0):
            mc = P.t(lambda E, m=m: E.matmul(pb[6][:, 0:1], sm_ones1r, sm_coef[0:1, m:m + 1],
                                             start=True, stop=True),
                     deps=[scoef, m_ones] + ([ccps[-1]] if ccps else [sb1]))
            ccps.append(P.s(lambda E, m=m: E.copy(sm_ccol[:, m:m + 1], pb[6][:, 0:1]), deps=[mc]))
        ccol_fin = ccps[-1] if full else None
        s_nlp = P.s(lambda E: E.mul(sm_lpc, sm_lpc, -1.0), deps=[c_all]) if full else None

        wtr = []
        for cchunk in range(8):
            cs = slice(cchunk * 128, (cchunk + 1) * 128)
            mh = P.t(lambda E, cs=cs: E.matmul(pb[7][:, 0:8], g_h8[:, cs], sm_ident[0:8, 0:8],
                                               start=True, stop=True),
                     deps=[g0_all, c_all] + ([wtr[-1]] if wtr else [sb2]))
            vh = P.v(lambda E: E.tensor_copy(gt(0), pb[7][:, 0:8]), deps=[mh])
            md = P.t(lambda E, cs=cs: E.matmul(pb[6][:, 0:8], g_dsl[:, cs], sm_ident[0:8, 0:8],
                                               start=True, stop=True),
                     deps=[vh] + ([] if wtr else [ccol_fin, sb1]))
            vd = P.v(lambda E: E.tensor_copy(gt(1), pb[6][:, 0:8]), deps=[md])
            P.v(lambda E: E.tensor_scalar_mul(gt(2), gt(1), sm_s2c[:, 0:1]), deps=[sb2])
            P.v(lambda E: E.scalar_tensor_tensor(gt(2), gt(0), sm_s1c[:, 0:1], gt(2),
                                                 ALU.mult, ALU.add))
            P.v(lambda E: E.reduce_max(gc(0), gt(2), axis=AX.X))
            vexp_in = P.v(lambda E: E.tensor_scalar(gt(3), gt(2), gc(0), None, ALU.subtract))
            sexp = P.s(lambda E: E.activation(gt(3), gt(3), AF.Exp), deps=[vexp_in])
            P.v(lambda E: E.reduce_sum(gc(1), gt(3), axis=AX.X), deps=[sexp])
            P.v(lambda E: E.reciprocal(gc(1), gc(1)))
            P.v(lambda E: E.tensor_scalar_mul(gt(4), gt(3), gc(1)))
            P.v(lambda E: E.reduce_max(gc(2), gt(4), axis=AX.X))
            P.v(lambda E: E.tensor_scalar(gt(5), gt(4), gc(2), None, ALU.is_ge))
            P.v(lambda E: E.reduce_sum(gc(3), gt(5), axis=AX.X))
            P.v(lambda E: E.scalar_tensor_tensor(gt(6), gt(5), -1e30, gt(4), ALU.mult, ALU.add))
            P.v(lambda E: E.reduce_max(gc(4), gt(6), axis=AX.X))
            P.v(lambda E: E.tensor_scalar(gc(5), gc(3), 2.0, None, ALU.is_ge))
            P.v(lambda E: E.tensor_sub(gc(6), gc(2), gc(4)))
            P.v(lambda E: E.tensor_mul(gc(6), gc(5), gc(6)))
            P.v(lambda E: E.tensor_add(gc(6), gc(6), gc(4)))
            vth = P.v(lambda E: E.tensor_scalar(gt(7), gt(4), gc(6), 2.0, ALU.subtract, ALU.mult))
            ssig = P.s(lambda E: E.activation(gt(7), gt(7), AF.Sigmoid), deps=[vth])
            P.v(lambda E: E.tensor_mul(gt(8), gt(4), gt(7)), deps=[ssig])
            P.v(lambda E: E.reduce_sum(gc(7), gt(8), axis=AX.X))
            P.v(lambda E: E.tensor_scalar_add(gc(7), gc(7), 1e-9))
            P.v(lambda E: E.reciprocal(gc(7), gc(7)))
            vw = P.v(lambda E: E.tensor_scalar_mul(gt(8), gt(8), gc(7)))
            mw = P.t(lambda E: E.matmul(pb[6][0:8, 0:128], gt(8), sm_ident[:, :],
                                        start=True, stop=True), deps=[vw])
            wtr.append(P.v(lambda E, cs=cs: E.tensor_copy(g_wT[:, cs], pb[6][0:8, 0:128]),
                           deps=[mw]))
        vwb = P.v(lambda E: E.tensor_copy(g_wTb[:, 0:NANCH], g_wT), deps=[wtr[-1]])
        pm_ev = []
        for half in range(2):
            mp = P.t(lambda E, half=half: E.matmul(
                pb[7][0:32, :], sm_limb, g_wTb[:, half * NCH:(half + 1) * NCH],
                start=True, stop=True), deps=[vwb, cw_all] + ([pm_ev[-1]] if pm_ev else []))
            pm_ev.append(P.v(lambda E, half=half: E.tensor_copy(
                g_pmix[:, half * NCH:(half + 1) * NCH], pb[7][0:32, :]), deps=[mp]))
        zpb_fin = None
        for par in range(2):
            zpb_fin = P.v(lambda E, par=par: E.scalar_tensor_tensor(
                sm_zpb[:, par::2], g_z1s[:, par::2], 0.5, g_pmix, ALU.mult, ALU.mult),
                deps=[pm_ev[-1], g0_all])

        p1_xres = [scr(0, NT), scr(2048, NT)]
        p1_obuf = scr(4096, NT)
        p1_wb = [scrB(6144, KD * 128), scrB(8192, KD * 128)]
        p1_Bmb = scrB(10240, DM, 32)

        ev_prev = {}
        st_prev = {}
        mmg = {}
        cw2_all = None
        deferred = []

        def tail(m, n, bank, i, xk):
            mmg[(m, n)] = P.t(lambda E, m=m, n=n, bank=bank: E.matmul(
                pb[bank][:, :], p1_Bmb[:, m * 128:(m + 1) * 128],
                sm_zpb[:, n * NCH:(n + 1) * NCH], start=False, stop=True),
                deps=[zpb_fin, cw2_all] if (m == 0 and n == 0) else [])
            sev = P.v(lambda E, m=m, n=n, bank=bank: E.tensor_scalar(
                p1_obuf[:, n * NCH:(n + 1) * NCH], pb[bank][:, :],
                sm_bcol[:, m:m + 1], 0.5, ALU.add, ALU.mult),
                deps=[mmg[(m, n)], c_all] + ([st_prev[m - 1]] if (n == 0 and m >= 1) else []))
            ev_prev[(m, n)] = P.v(lambda E, m=m, n=n, i=i: E.tensor_add(
                p1_obuf[:, n * NCH:(n + 1) * NCH], p1_obuf[:, n * NCH:(n + 1) * NCH],
                p1_xres[i][:, n * NCH:(n + 1) * NCH]),
                deps=[xk])
            if n == NNC - 1:
                st_prev[m] = dma("sync", f"sx{m % 2}", lambda E, m=m: E.dma_start(
                    out=x1d[m * 128:(m + 1) * 128, :], in_=p1_obuf[:, :]),
                    deps=[ev_prev[(m, NNC - 1)]])

        for m in range(KD):
            i = m % 2
            wdep = [mmg[(m - 2, NNC - 1)]] if m >= 2 else []
            wk = dma("gpsimd", f"w{i}", lambda E, m=m, i=i: E.dma_start(
                out=p1_wb[i][:, :], in_=Wp[m]), deps=wdep)
            xk = dma("sync", f"x{i}", lambda E, m=m, i=i: E.dma_start(
                out=p1_xres[i][:, :], in_=x[m * 128:(m + 1) * 128, :]),
                deps=[ev_prev.get((m - 2, NNC - 1))] if m >= 2 else [zpb_fin])
            for n in range(NNC):
                bank = (m * NNC + n) % 8
                mmdeps = []
                if m == 0 and n == 0:
                    mmdeps = lnr_all + [wk, c_all, m_ones2]
                elif n == 0:
                    mmdeps = [wk]
                if (m * NNC + n) >= 8:
                    prev_grp = m * NNC + n - 8
                    mmdeps.append(ev_prev.get((prev_grp // NNC, prev_grp % NNC)))
                elif bank >= 6:
                    mmdeps.append(pm_ev[-1])
                for k in range(KD):
                    P.t(lambda E, m=m, n=n, k=k, i=i, bank=bank: E.matmul(
                        pb[bank][:, :], p1_wb[i][:, k * 128:(k + 1) * 128],
                        lnres(k)[:, n * NCH:(n + 1) * NCH],
                        start=(k == 0), stop=False), deps=mmdeps if k == 0 else [])
                if m < 2:
                    deferred.append((m, n, bank, i, xk))
                else:
                    tail(m, n, bank, i, xk)
            if m == 1:
                dma("gpsimd", "cw2", lambda E: E.dma_start(out=p1_Bmb, in_=Bm[:, :]),
                    deps=[zpb_fin])
                cw2_all = P.now("cw2")
                for (dm, dn, dbank, di, dxk) in deferred:
                    tail(dm, dn, dbank, di, dxk)
        ph1_done_st = [P.now("sx0"), P.now("sx1")]
        ph1_done_mm = P.now("mm")

        if phases <= 1 or not full:
            P.wait("sync", "sx0", C["sx0"])
            P.wait("sync", "sx1", C["sx1"])
            semnames = ["c", "cw", "cw2", "lnr0", "lnr1", "g0", "w0", "w1", "x0", "x1",
                        "sx0", "sx1", "ov", "os", "mm"]
            _finalize(nc, P, es, semnames)
            return nc

        ph2 = emit_prepass(nc, P, pb, scrF, smla, x1d, lnto_big=lnres,
                           deps_start=[ph1_done_st[0], ph1_done_st[1], ph1_done_mm,
                                       ("ov", C["ov"]), c_all],
                           tag="p2", sm_onesc=sm_onesc, sm_ones1r=sm_ones1r,
                           sm_onescb=sm_onescb)

        ug = [scrB(0, KD * 2 * 128), scrB(4096, KD * 2 * 128)]
        sA = scr(8192, NCH)
        sAb = scrB(8192, NCH)
        sB = scr(8704, NCH)
        sC = scr(9216, NCH)
        sD = scr(9728, NCH)
        sE = scr(10240, NCH)
        sF = scr(10752, NCH)
        tstg = [scrB(11264, NCH), scrB(11520, NCH), None, None, None, None]
        ph2done = ph2["done"]
        tstg[2] = smla[0:128, 1840:2096].bitcast(BF16)
        tstg[3] = smla[0:128, 2096:2352].bitcast(BF16)
        tstg[4] = smla[0:128, 2352:2608].bitcast(BF16)
        tstg[5] = smla[0:128, 2608:2864].bitcast(BF16)
        mm_ph3 = {}
        ev_ph3 = {}
        tst_cnt = [0] * 6
        au_st = {}
        for ms in range(8):
            iu = ms % 2
            wdeps = [ph2done] if ms < 2 else [mm_ph3[(ms - 2, NNC - 1)]]
            ugk = P.dma("gpsimd", f"ug{iu}", lambda E, ms=ms, iu=iu: E.dma_start(
                out=ug[iu][:, :], in_=UGp[ms]), deps=wdeps)
            for n in range(NNC):
                j = ms * NNC + n
                bau, bgp = (2 * j) % 8, (2 * j + 1) % 8
                mmdeps = [ugk, ph2done] if n == 0 else []
                if j >= 4:
                    pj = j - 4
                    mmdeps.append(ev_ph3[(pj // NNC, pj % NNC)])
                    mmdeps.append(ev_ph3[(pj // NNC, pj % NNC, "s")])
                for k in range(KD):
                    P.t(lambda E, k=k, n=n, iu=iu, bau=bau: E.matmul(
                        pb[bau][:, :], ug[iu][:, k * 128:(k + 1) * 128],
                        lnres(k)[:, n * NCH:(n + 1) * NCH],
                        start=(k == 0), stop=(k == KD - 1)), deps=mmdeps if k == 0 else [])
                for k in range(KD):
                    mm_ph3[(ms, n)] = P.t(lambda E, k=k, n=n, iu=iu, bgp=bgp: E.matmul(
                        pb[bgp][:, :], ug[iu][:, (KD + k) * 128:(KD + k + 1) * 128],
                        lnres(k)[:, n * NCH:(n + 1) * NCH],
                        start=(k == 0), stop=(k == KD - 1)))
                mmj = mm_ph3[(ms, n)]
                vA = P.s(lambda E, bau=bau: E.copy(sAb, pb[bau][:, :]),
                         deps=[mmj, au_st.get((ms, n - 1)) if n >= 1
                               else au_st.get((ms - 1, NNC - 1))])
                au_st[(ms, n)] = P.dma("scalar", "ha", lambda E, ms=ms, n=n: E.dma_start(
                    out=HID[ms][:, n * NCH:(n + 1) * NCH], in_=sAb), deps=[vA])
                vB = P.v(lambda E, ms=ms, bgp=bgp: E.tensor_scalar(
                    sB, pb[bgp][:, :], sm_lpc[:, ms:ms + 1], None, ALU.add),
                    deps=[mmj] + ([s_nlp] if (ms == 0 and n == 0) else []))
                sab = P.s(lambda E: E.activation(sC, sB, AF.Abs), deps=[("ov", C["ov"])])
                P.v(lambda E: E.tensor_scalar(sC, sC, 2.5, None, ALU.is_le), deps=[sab])
                P.v(lambda E: E.tensor_mul(sD, sB, sB))
                P.v(lambda E: E.tensor_mul(sE, sD, sB))
                def tmpst(slot, dsrc, mi, ausrc, extra=()):
                    tdep = [(f"t{slot}", tst_cnt[slot] * 16)] if tst_cnt[slot] else []
                    vt = P.v(lambda E, dsrc=dsrc, mi=mi, ausrc=ausrc, slot=slot:
                             E.scalar_tensor_tensor(tstg[slot][:, 0:NCH], dsrc, sm_ccol[:, mi:mi + 1],
                                                    ausrc, ALU.mult, ALU.mult),
                             deps=list(extra) + tdep + ([ccol_fin] if (ms == 0 and n == 0 and mi == 0) else []))
                    tst_cnt[slot] += 1
                    return P.dma("sync", f"t{slot}", lambda E, mi=mi, ms=ms, n=n, slot=slot:
                                 E.dma_start(out=HID[8 + mi * 8 + ms][:, n * NCH:(n + 1) * NCH],
                                             in_=tstg[slot][:, 0:NCH]), deps=[vt])
                tmpst(0, sB, 0, pb[bau][:, :])
                vB2 = P.v(lambda E, bau=bau: E.tensor_mul(sB, sC, pb[bau][:, :]))
                tmpst(1, sD, 1, pb[bau][:, :])
                tmpst(2, sE, 2, pb[bau][:, :])
                P.v(lambda E: E.tensor_mul(sF, sD, sD))
                tmpst(3, sF, 3, pb[bau][:, :])
                P.v(lambda E: E.tensor_mul(sF, sD, sE))
                tmpst(4, sF, 4, sB, extra=[vB2])
                P.v(lambda E: E.tensor_mul(sF, sE, sE))
                tmpst(5, sF, 5, sB)
                ev_ph3[(ms, n)] = ("ov", C["ov"])
                ev_ph3[(ms, n, "s")] = ("os", C["os"])

        gb = [scrB(0, KD * 128), scrB(2048, KD * 128)]
        ub = [scrB(4096, KD * 128), scrB(6144, KD * 128)]
        sg = scr(8192, NCH)
        hstg = [scrB(8704, NCH), scrB(8960, NCH)]
        NF = FF // 128
        ph3_mm_done = P.now("mm")
        ph3_ov_done = ("ov", C["ov"])
        mm_ph4 = {}
        hv_ph4 = {}
        hst4 = {}
        for mf in range(NF):
            i = mf % 2
            wdeps = [ph3_ov_done, ph3_mm_done, ("ha", C["ha"])] if mf < 2 else [mm_ph4[(mf - 2, NNC - 1)]]
            gk = P.dma("gpsimd", f"fg{i}", lambda E, mf=mf, i=i: E.dma_start(
                out=gb[i][:, :], in_=FGp[mf]), deps=wdeps)
            uk = P.dma("gpsimd", f"fu{i}", lambda E, mf=mf, i=i: E.dma_start(
                out=ub[i][:, :], in_=FUp[mf]))
            for n in range(NNC):
                j = mf * NNC + n
                bg, bu = (2 * j) % 8, (2 * j + 1) % 8
                mmdeps = [gk, uk, ph3_mm_done] if n == 0 else []
                if j >= 4:
                    pj = j - 4
                    mmdeps.append(hv_ph4[(pj // NNC, pj % NNC)])
                for k in range(KD):
                    P.t(lambda E, k=k, n=n, i=i, bg=bg: E.matmul(
                        pb[bg][:, :], gb[i][:, k * 128:(k + 1) * 128],
                        lnres(k)[:, n * NCH:(n + 1) * NCH],
                        start=(k == 0), stop=(k == KD - 1)), deps=mmdeps if k == 0 else [])
                for k in range(KD):
                    mm_ph4[(mf, n)] = P.t(lambda E, k=k, n=n, i=i, bu=bu: E.matmul(
                        pb[bu][:, :], ub[i][:, k * 128:(k + 1) * 128],
                        lnres(k)[:, n * NCH:(n + 1) * NCH],
                        start=(k == 0), stop=(k == KD - 1)))
                mmj = mm_ph4[(mf, n)]
                ssil = P.s(lambda E, bg=bg: E.activation(sg, pb[bg][:, :], AF.Silu),
                           deps=[mmj, hv_ph4.get((mf, n - 1)) if n >= 1
                                 else hv_ph4.get((mf - 1, NNC - 1))])
                hv_ph4[(mf, n)] = P.v(lambda E, n=n, bu=bu: E.tensor_mul(
                    hstg[n % 2][:, :], sg, pb[bu][:, :]),
                    deps=[ssil, hst4.get((mf, n - 2)) or hst4.get((mf - 1, n + 2))])
                hst4[(mf, n)] = P.dma("sync", f"h{n % 2}", lambda E, mf=mf, n=n: E.dma_start(
                    out=HID[56 + mf][:, n * NCH:(n + 1) * NCH], in_=hstg[n % 2][:, :]),
                    deps=[hv_ph4[(mf, n)]])

        wb5 = [scrB(0, NWH * 128), scrB(4288, NWH * 128)]
        x1c = [scr(8576, NCH), scr(9088, NCH)]
        oc = [scr(9600, NCH), scr(10112, NCH)]
        ph4_mm_done = P.now("mm")
        ph4_st_done = [("h0", C["h0"]), ("h1", C["h1"]), ("ha", C["ha"])]
        ph3_t_done = [(f"t{s}", C[f"t{s}"]) for s in range(6)]
        mm_ph5 = {}
        ev_ph5 = {}
        st_ph5 = {}
        wld5 = {}
        for n in range(NNC):
            hdeps = ([ph4_mm_done] + ph4_st_done + ph3_t_done) if n == 0 else [mm_ph5[(n - 1, KD - 1)]]
            hchunk = {}
            for kk in range(NK_BIG):
                g = kk // 16
                P.dma("sync", f"hd{n % 2}c{g}", lambda E, kk=kk, n=n: E.dma_start(
                    out=hidres(kk), in_=HID[kk][:, n * NCH:(n + 1) * NCH]),
                    deps=hdeps if kk == 0 else [])
                hchunk[g] = P.now(f"hd{n % 2}c{g}")
            for mo in range(KD):
                jj5 = n * KD + mo
                wdeps = [ph4_mm_done, ("ov", C["ov"])] if jj5 == 0 else []
                wA = P.dma("gpsimd", "wbA", lambda E, mo=mo: E.dma_start(
                    out=wb5[0][:, :], in_=WB[mo][:, 0:NWH * 128]),
                    deps=wdeps + [("mmh0", C["mmh0"])])
                wB = P.dma("gpsimd", "wbB", lambda E, mo=mo: E.dma_start(
                    out=wb5[1][:, :], in_=WB[mo][:, NWH * 128:NK_BIG * 128]),
                    deps=wdeps + [("mmh1", C["mmh1"])])
                xk5 = P.dma("sync", f"y{mo % 2}", lambda E, mo=mo, n=n: E.dma_start(
                    out=x1c[mo % 2][:, :], in_=x1d[mo * 128:(mo + 1) * 128, n * NCH:(n + 1) * NCH]),
                    deps=[ev_ph5.get(((jj5 - 2) // KD, (jj5 - 2) % KD)) if jj5 >= 2 else None]
                         + ([ph4_mm_done, ("ov", C["ov"])] if jj5 < 2 else []))
                bank = pb[jj5 % 4 + 4]
                for kk in range(NK_BIG):
                    half = 0 if kk < NWH else 1
                    wdep = []
                    if mo == 0 and kk % 16 == 0:
                        wdep.append(hchunk[kk // 16])
                    if kk == 0:
                        wdep.append(wA)
                        if mo != 0:
                            wdep.append(hchunk[(NK_BIG - 1) // 16])
                        if jj5 >= 4:
                            q = jj5 - 4
                            wdep.append(ev_ph5[(q // KD, q % KD)])
                    elif kk == NWH:
                        wdep.append(wB)
                    inc = "mmh0" if kk == NWH - 1 else ("mmh1" if kk == NK_BIG - 1 else "mm")
                    mm_ph5[(n, mo)] = P.t(lambda E, kk=kk, half=half, bank=bank: E.matmul(
                        bank[:, :], wb5[half][:, (kk - half * NWH) * 128:(kk - half * NWH + 1) * 128],
                        hidres(kk), start=(kk == 0), stop=(kk == NK_BIG - 1)),
                        deps=wdep, inc=inc)
                pstq = jj5 - 2
                ev_ph5[(n, mo)] = P.v(lambda E, mo=mo, bank=bank: E.tensor_add(
                    oc[mo % 2][:, :], x1c[mo % 2][:, :], bank[:, :]),
                    deps=[mm_ph5[(n, mo)], xk5,
                          st_ph5.get((pstq // KD, pstq % KD)) if pstq >= 0 else None])
                st_ph5[(n, mo)] = P.dma("sync", f"z{mo % 2}", lambda E, mo=mo, n=n: E.dma_start(
                    out=x2o[mo * 128:(mo + 1) * 128, n * NCH:(n + 1) * NCH], in_=oc[mo % 2][:, :]),
                    deps=[ev_ph5[(n, mo)]])
        ph5_done = [P.now("z0"), P.now("z1"), P.now("mm"), P.now("mmh0"), P.now("mmh1"),
                    ("ov", C["ov"])]

        WAb6 = smla[0:128, 816:1456].bitcast(BF16)
        wald = P.dma("gpsimd", "wa6", lambda E: E.dma_start(out=WAb6, in_=WA[0]))
        wa6 = {
            "WAb": WAb6, "ld": (wald[0], C["wa6"]), "bAs": sm_bAs, "onesN": sm_onesN,
            "limTs": sm_limT, "ident": sm_ident,
            "zH": scrF[0:40, 0:NT], "h8s": scrF[0:8, NT:NT + NANCH],
            "dsls": scrF[0:8, NT + NANCH:2 * NT], "parts": sm_parts,
            "hm8": smla[0:8, 2938:2939], "dm8": smla[0:8, 2940:2941],
            "hm1": smla[0:1, 2988:2996], "dm1": smla[0:1, 2996:3004],
            "out_z": z3[:, :], "out_h8a": h8a3[:, :], "out_dsl": dsl3[:, :],
            "out_parts": partials3[:, :],
        }
        ph6 = emit_prepass(nc, P, pb, scrF, smla, x2o, lnto_big=lnres,
                           deps_start=ph5_done + [c_all],
                           tag="p6", sm_onesc=sm_onesc, sm_ones1r=sm_ones1r,
                           wa=wa6, store_lnx=lnx3, lnbig_free=ph5_done[2],
                           sm_onescb=sm_onescb)

        P.wait("sync", "p6o", C["p6o"])
        P.wait("sync", "p6s0", C["p6s0"])
        P.wait("sync", "p6s1", C["p6s1"])
        P.wait("sync", "z0", C["z0"])
        P.wait("sync", "z1", C["z1"])

        semnames = ["c", "cw", "cw2", "lnr0", "lnr1", "g0", "w0", "w1", "x0", "x1",
                    "sx0", "sx1", "ov", "os", "mm", "p2x0", "p2x1",
                    "ug0", "ug1", "t0", "t1", "t2", "t3", "t4", "t5", "ha",
                    "fg0", "fg1", "fu0", "fu1",
                    "h0", "h1", "wbA", "wbB", "mmh0", "mmh1",
                    *[f"hd{pp}c{gg}" for pp in range(2) for gg in range(9)],
                    "y0", "y1", "z0", "z1", "wa6", "p6x0", "p6x1", "p6s0", "p6s1", "p6o"]
        _finalize(nc, P, es, semnames)
    return nc


def emit_prepass(nc, P, pb, scrF, smla, src_dram, lnto_big, deps_start, tag,
                 sm_onesc, sm_ones1r=None, wa=None, store_lnx=None, lnbig_free=None,
                 sm_onescb=None):
    C = P.C
    xb = [scrF[:, 0:NT], scrF[:, NT:2 * NT]]
    sq = [scrF[:, 2 * NT:3 * NT], scrF[:, 3 * NT:4 * NT]]
    sqb = [scrF[:, 2 * NT:2 * NT + NT // 2].bitcast(BF16),
           scrF[:, 3 * NT:3 * NT + NT // 2].bitcast(BF16)]
    Ba = scrF[:, 4 * NT:5 * NT]
    Bb = scrF[:, 5 * NT:6 * NT]
    r_s = scrF[0:1, 0:NT]
    r_sq = scrF[0:1, NT:2 * NT]
    r_a = scrF[0:1, 2 * NT:3 * NT]
    r_b = scrF[0:1, 3 * NT:4 * NT]
    vprev = None
    sqd = {}
    mmst = {}
    for k in range(KD):
        i = k % 2
        deps = list(deps_start) if k < 2 else []
        if k >= 2:
            deps.append(mmst[k - 2])
        xk = P.dma("sync" if i == 0 else "gpsimd", f"{tag}x{i}", lambda E, k=k, i=i: E.dma_start(
            out=xb[i][:, :], in_=src_dram[k * 128:(k + 1) * 128, :]),
            deps=deps if k < 2 else deps)
        sqd[k] = P.s(lambda E, i=i: E.activation(sqb[i][:, :], xb[i][:, :], AF.Square),
                     deps=[xk] + ([mmst[k - 2]] if k >= 2 else []))
        for n in range(NNC):
            P.t(lambda E, k=k, n=n, i=i: E.matmul(
                pb[n][0:1, :], sm_onesc, xb[i][:, n * NCH:(n + 1) * NCH],
                start=(k == 0), stop=(k == KD - 1)),
                deps=[xk] if n == 0 else [])
        for n in range(NNC):
            mmst[k] = P.t(lambda E, k=k, n=n, i=i: E.matmul(
                pb[4 + n][0:1, :], sm_onescb, sqb[i][:, n * NCH:(n + 1) * NCH],
                start=(k == 0), stop=(k == KD - 1)),
                deps=[sqd[k]] if n == 0 else [])
    st_done = mmst[KD - 1]
    evs = []
    for n in range(NNC):
        evs.append(P.v(lambda E, n=n: E.tensor_copy(r_s[:, n * NCH:(n + 1) * NCH], pb[n][0:1, :]),
                       deps=[st_done] if n == 0 else []))
        evs.append(P.v(lambda E, n=n: E.tensor_copy(r_sq[:, n * NCH:(n + 1) * NCH], pb[4 + n][0:1, :])))
    P.v(lambda E: E.tensor_scalar_mul(r_s, r_s, 1.0 / DM))
    P.v(lambda E: E.tensor_scalar_mul(r_sq, r_sq, 1.0 / DM))
    P.v(lambda E: E.tensor_mul(r_a, r_s, r_s))
    P.v(lambda E: E.tensor_sub(r_sq, r_sq, r_a))
    veps = P.v(lambda E: E.tensor_scalar_add(r_sq, r_sq, 1e-5))
    sln = P.s(lambda E: E.activation(r_sq, r_sq, AF.Ln), deps=[veps])
    sexp = P.s(lambda E: E.activation(r_a, r_sq, AF.Exp, scale=-0.5), deps=[sln])
    vbr = P.v(lambda E: E.scalar_tensor_tensor(r_b, r_s, -1.0, r_a, ALU.mult, ALU.mult),
              deps=[sexp])
    prevb = {}
    jj = 0
    for (row, dst, dep) in ((r_a, Ba, sexp), (r_b, Bb, vbr)):
        for n in range(NNC):
            bank = pb[jj % 2]
            deps = [dep] + ([prevb[jj - 2]] if jj >= 2 else [evs[-1]])
            mmb = P.t(lambda E, row=row, n=n, bank=bank: E.matmul(
                bank[:, :], sm_ones1r, row[:, n * NCH:(n + 1) * NCH], start=True, stop=True),
                deps=deps)
            prevb[jj] = P.s(lambda E, dst=dst, n=n, bank=bank: E.copy(
                dst[:, n * NCH:(n + 1) * NCH], bank[:, :]), deps=[mmb])
            jj += 1
    bcf = prevb[jj - 1]
    vln = {}
    mm_wa = {}
    stl = {}
    for k in range(KD):
        i = k % 2
        bdeps = [vln[k - 2]] if k >= 2 else [st_done, bcf, prevb[jj - 1]]
        if wa is not None and k >= 2:
            bdeps.append(mm_wa[k - 2])
        if store_lnx is not None and k >= 2:
            bdeps.append(stl[k - 2])
        xk = P.dma("sync" if i == 0 else "gpsimd", f"{tag}x{i}", lambda E, k=k, i=i: E.dma_start(
            out=xb[i][:, :], in_=src_dram[k * 128:(k + 1) * 128, :]), deps=bdeps)
        P.v(lambda E, i=i: E.tensor_mul(sq[i][:, :], xb[i][:, :], Ba[:, :]),
            deps=[xk, bcf] + ([lnbig_free] if (k < 2 and lnbig_free is not None) else []))
        vln[k] = P.v(lambda E, k=k, i=i: E.tensor_add(lnto_big(k), sq[i][:, :], Bb[:, :]))
        if wa is not None:
            for n in range(NNC):
                mm_wa[k] = P.t(lambda E, k=k, n=n: E.matmul(
                    pb[4 + n][0:40, :], wa["WAb"][:, k * 40:(k + 1) * 40],
                    lnto_big(k)[:, n * NCH:(n + 1) * NCH],
                    start=(k == 0), stop=False),
                    deps=[vln[k], wa["ld"]] if n == 0 else [])
        if store_lnx is not None:
            stl[k] = P.dma("sync", f"{tag}s{i}", lambda E, k=k: E.dma_start(
                out=store_lnx[k * 128:(k + 1) * 128, :], in_=lnto_big(k)), deps=[vln[k]])
    res = {"done": vln[KD - 1], "vln": vln}
    if store_lnx is not None:
        res["st"] = [P.now(f"{tag}s0"), P.now(f"{tag}s1")]
    if wa is None:
        return res
    for n in range(NNC):
        P.t(lambda E, n=n: E.matmul(pb[4 + n][0:40, :], wa["bAs"], wa["onesN"],
                                    start=False, stop=True))
    mmdone = P.now("mm")
    zH, h8s, dsls, parts = wa["zH"], wa["h8s"], wa["dsls"], wa["parts"]
    ez = None
    for n in range(NNC):
        ez = P.v(lambda E, n=n: E.tensor_copy(zH[:, n * NCH:(n + 1) * NCH], pb[4 + n][0:40, :]),
                 deps=[mmdone] if n == 0 else [])
    vh8 = P.v(lambda E: E.tensor_copy(h8s[:, :], zH[32:40, 1::2]))
    dsl_ev = None
    for half in range(2):
        mmd = P.t(lambda E, half=half: E.matmul(
            pb[0][0:8, :], wa["limTs"], zH[0:32, 1 + half * 1024:min(1 + half * 1024 + 1024, NT):2],
            start=True, stop=True), deps=[ez] if half == 0 else [dsl_ev])
        dsl_ev = P.v(lambda E, half=half: E.tensor_copy(
            dsls[:, half * NCH:(half + 1) * NCH], pb[0][0:8, :]), deps=[mmd])
    hm8, dm8, hm1, dm1 = wa["hm8"], wa["dm8"], wa["hm1"], wa["dm1"]
    vm1 = P.v(lambda E: E.reduce_max(hm8, h8s[:, :], axis=AX.X, apply_absolute_value=True),
              deps=[vh8])
    vm2 = P.v(lambda E: E.reduce_max(dm8, dsls[:, :], axis=AX.X, apply_absolute_value=True),
              deps=[dsl_ev])
    mt1 = P.t(lambda E: E.matmul(pb[1][0:1, 0:8], hm8, wa["ident"][0:8, 0:8],
                                 start=True, stop=True), deps=[vm1])
    vt1 = P.v(lambda E: E.tensor_copy(hm1, pb[1][0:1, 0:8]), deps=[mt1])
    mt2 = P.t(lambda E: E.matmul(pb[2][0:1, 0:8], dm8, wa["ident"][0:8, 0:8],
                                 start=True, stop=True), deps=[vm2])
    vt2 = P.v(lambda E: E.tensor_copy(dm1, pb[2][0:1, 0:8]), deps=[mt2])
    P.v(lambda E: E.reduce_max(parts[:, 0:1], hm1, axis=AX.X), deps=[vt1])
    vparts = P.v(lambda E: E.reduce_max(parts[:, 1:2], dm1, axis=AX.X), deps=[vt2])
    P.dma("sync", f"{tag}o", lambda E: E.dma_start(out=wa["out_z"], in_=zH[0:32, :]), deps=[ez])
    P.dma("sync", f"{tag}o", lambda E: E.dma_start(out=wa["out_h8a"], in_=h8s[:, :]), deps=[vh8])
    P.dma("sync", f"{tag}o", lambda E: E.dma_start(out=wa["out_dsl"], in_=dsls[:, :]), deps=[dsl_ev])
    P.dma("sync", f"{tag}o", lambda E: E.dma_start(out=wa["out_parts"], in_=parts), deps=[vparts])
    res["out_st"] = P.now(f"{tag}o")
    return res
